# version 01 (final)
# baseline (speedup 1.0000x reference)
import numpy as np
import ml_dtypes
from contextlib import ExitStack

import concourse.bass as bass
import concourse.mybir as mybir
from concourse.bass_utils import run_bass_kernel_spmd

F32 = mybir.dt.float32
BF16 = mybir.dt.bfloat16
F16 = mybir.dt.float16
U8 = mybir.dt.uint8
I32 = mybir.dt.int32
AF = mybir.ActivationFunctionType
ALU = mybir.AluOpType
AX = mybir.AxisListType

B, L, D = 4, 4096, 2048
TOK = 2048
NKD = D // 128
LN_EPS = 1e-5


def tile_w(w, pad_to=None):
    O, K = w.shape
    if pad_to is not None and O < pad_to:
        w = np.concatenate([w, np.zeros((pad_to - O, K), w.dtype)], axis=0)
        O = pad_to
    assert O % 128 == 0 and K % 128 == 0, (O, K)
    t = w.reshape(O // 128, 128, K // 128, 128)
    return np.ascontiguousarray(t.transpose(0, 3, 2, 1))


def tile_vec(v, pad_to=None):
    if pad_to is not None and v.shape[0] < pad_to:
        v = np.concatenate([v, np.zeros(pad_to - v.shape[0], v.dtype)])
    n = v.shape[0] // 128
    return np.ascontiguousarray(v.reshape(n, 128).T)


class Ctr:
    def __init__(self):
        self.v = 0

    def inc(self, n=1):
        self.v += n
        return self.v


class Prog:
    def __init__(self, name):
        self.nc = bass.Bass("TRN2", target_bir_lowering=False)
        self.es = ExitStack()
        self.name = name
        self.sems = {}
        self.ctrs = {}

    def sbuf(self, name, shape, dt=F32):
        return self.es.enter_context(self.nc.sbuf_tensor(name, shape, dt))

    def psum(self, name, shape, dt=F32):
        return self.es.enter_context(self.nc.psum_tensor(name, shape, dt))

    def sem(self, name):
        if name not in self.sems:
            self.sems[name] = self.es.enter_context(self.nc.semaphore(name))
            self.ctrs[name] = Ctr()
        return self.sems[name]

    def ctr(self, name):
        self.sem(name)
        return self.ctrs[name]

    def begin(self):
        self.block = self.es.enter_context(self.nc.Block())
        return self.block

    def close(self):
        self.es.close()


def _waits(eng, waits):
    for s, v in waits:
        if v > 0:
            eng.wait_ge(s, v)


def emit_ln(p, x_view, nk, width, out_view, sq_sb, stat_ps, bc_ps, stat_sb,
            ones_k, ones_1, ndim, waits_pre=(), eps=LN_EPS):
    blk = p.block
    sv, ss, sm = p.sem("sv"), p.sem("ss"), p.sem("sm")
    cv, cs, cm = p.ctr("sv"), p.ctr("ss"), p.ctr("sm")
    lnq = p.sem("lnq"); cq = p.ctr("lnq")
    W = width
    assert W <= 512
    ss_base = cs.v
    lnq_base = cq.v

    @blk.scalar
    def _(scalar):
        _waits(scalar, waits_pre)
        for k in range(nk):
            if k >= 2:
                scalar.wait_ge(lnq, lnq_base + k - 1)
            scalar.activation(sq_sb[:, (k % 2) * 512:(k % 2) * 512 + W], x_view(k),
                              AF.Square).then_inc(ss, 1)
    cs.inc(nk)

    @blk.tensor
    def _(tensor):
        _waits(tensor, waits_pre)
        for k in range(nk):
            tensor.matmul(stat_ps[0][0:1, 0:W], ones_k[:, 0:1], x_view(k),
                          start=(k == 0), stop=(k == nk - 1))
        ins = None
        for k in range(nk):
            tensor.wait_ge(ss, ss_base + k + 1)
            ins = tensor.matmul(stat_ps[1][0:1, 0:W], ones_k[:, 0:1],
                                sq_sb[:, (k % 2) * 512:(k % 2) * 512 + W],
                                start=(k == 0), stop=(k == nk - 1))
            if k < nk - 1:
                ins.then_inc(lnq, 1)
                cq.inc()
        ins.then_inc(sm, 1)
    stats_done = cm.inc()

    mu = stat_sb[0:1, 0:W]
    msq = stat_sb[0:1, 512:512 + W]
    var = stat_sb[0:1, 1024:1024 + W]
    rstd = stat_sb[0:1, 1536:1536 + W]
    mr = stat_sb[0:1, 2048:2048 + W]
    tmp = stat_sb[0:1, 2560:2560 + W]

    @blk.vector
    def _(vector):
        vector.wait_ge(sm, stats_done)
        vector.tensor_scalar(mu, stat_ps[0][0:1, 0:W], 1.0 / ndim, None, ALU.mult)
        vector.tensor_scalar(msq, stat_ps[1][0:1, 0:W], 1.0 / ndim, None, ALU.mult)
        vector.tensor_tensor(tmp, mu, mu, ALU.mult)
        vector.scalar_tensor_tensor(var, msq, eps, tmp,
                                    ALU.add, ALU.subtract).then_inc(sv, 1)
    var_done = cv.inc()

    @blk.scalar
    def _(scalar):
        scalar.wait_ge(sv, var_done)
        scalar.activation(tmp, var, AF.Sqrt).then_inc(ss, 1)
    sqrt_done = cs.inc()

    @blk.vector
    def _(vector):
        vector.wait_ge(ss, sqrt_done)
        vector.reciprocal(rstd, tmp)
        vector.tensor_tensor(mr, mu, rstd, ALU.mult).then_inc(sv, 1)
    mr_done = cv.inc()

    @blk.tensor
    def _(tensor):
        tensor.wait_ge(sv, mr_done)
        tensor.matmul(bc_ps[:, 0:W], ones_1[0:1, :], rstd, start=True, stop=True)
        tensor.matmul(bc_ps[:, 512:512 + W], ones_1[0:1, :], mr,
                      start=True, stop=True).then_inc(sm, 1)
    bc_done = cm.inc()

    @blk.vector
    def _(vector):
        vector.wait_ge(sm, bc_done)
        for k in range(nk):
            vector.tensor_tensor(sq_sb[:, (k % 2) * 512:(k % 2) * 512 + W], x_view(k),
                                 bc_ps[:, 0:W], ALU.mult)
            vector.tensor_tensor(out_view(k), sq_sb[:, (k % 2) * 512:(k % 2) * 512 + W],
                                 bc_ps[:, 512:512 + W], ALU.subtract).then_inc(sv, 1)
    return cv.inc(nk)


def build_l0a():
    p = Prog("l0a")
    nc = p.nc
    HW = 2051
    xh = nc.dram_tensor("xh", [D, HW], F32, kind="ExternalInput").ap()
    w_in = nc.dram_tensor("w_in", [64, 128, NKD, 128], F32, kind="ExternalInput").ap()
    w_x = nc.dram_tensor("w_x", [128, 32, 160], F32, kind="ExternalInput").ap()
    w_dt = nc.dram_tensor("w_dt", [128, 32, 128], F32, kind="ExternalInput").ap()
    cw = nc.dram_tensor("cw", [128, 32, 4], F32, kind="ExternalInput").ap()
    cb = nc.dram_tensor("cb", [128, 32], F32, kind="ExternalInput").ap()
    bdt = nc.dram_tensor("bdt", [128, 32], F32, kind="ExternalInput").ap()
    u_o = nc.dram_tensor("u_o", [4096, TOK], BF16, kind="ExternalOutput").ap()
    z_o = nc.dram_tensor("z_o", [4096, TOK], F32, kind="ExternalOutput").ap()
    dt_o = nc.dram_tensor("dt_o", [4096, TOK], BF16, kind="ExternalOutput").ap()
    bc_o = nc.dram_tensor("bc_o", [32, TOK], F32, kind="ExternalOutput").ap()
    import os
    dbg = bool(os.environ.get("DBG_XS"))
    xs_o = nc.dram_tensor("xs_o", [4096, 2051], F32, kind="ExternalOutput").ap() if dbg else None

    xch = p.sbuf("xch", [128, NKD, 512], F32)
    sqb = p.sbuf("sqb", [128, 2 * 512], F32)
    xnb = p.sbuf("xnb", [128, NKD, HW], BF16)
    stat_sb = p.sbuf("stat", [1, 6 * 512], F32)
    ones_k = p.sbuf("ones_k", [128, 1], F32)
    ones_1 = p.sbuf("ones_1", [1, 128], F32)
    wsl = [p.sbuf(f"wsl{i}", [128, NKD * 128], BF16) for i in range(2)]
    wxs = p.sbuf("wxs", [128, 32, 160], BF16)
    wdts = p.sbuf("wdts", [128, 32, 128], BF16)
    cwsb = p.sbuf("cwsb", [128, 32, 4], F32)
    cbsb = p.sbuf("cbsb", [128, 32], F32)
    bdtsb = p.sbuf("bdtsb", [128, 32], F32)
    xst = [p.sbuf(f"xst{i}", [128, HW], F32) for i in range(2)]
    ub = [p.sbuf(f"ub{i}", [128, 2048], BF16) for i in range(2)]
    cacc = [p.sbuf(f"cacc{i}", [128, 512], F32) for i in range(3)]
    cfin = [p.sbuf(f"cfin{i}", [128, 512], F32) for i in range(2)]
    uld = ub
    zst = [p.sbuf(f"zst{i}", [128, 512], F32) for i in range(4)]
    dtrb = p.sbuf("dtrb", [128, 2048], BF16)
    bcst = [p.sbuf(f"bcst{i}", [32, 512], F32) for i in range(2)]
    sp = [p.sbuf(f"sp{i}", [128, 512], F32) for i in range(8)]
    dtst = [p.sbuf(f"dtst{i}", [128, 512], BF16) for i in range(2)]

    bc_ps = p.psum("bc_ps", [128, 1024], F32)
    pj = [p.psum(f"pj{i}", [128, 512], F32) for i in range(6)]
    stat_ps = (pj[0], pj[1])

    ld = p.sem("ld"); cl = p.ctr("ld")
    sg = p.sem("sg"); cg = p.ctr("sg")
    wl = [p.sem(f"wl{i}") for i in range(2)]
    sv, ss, sm = p.sem("sv"), p.sem("ss"), p.sem("sm")
    cv, cs, cm = p.ctr("sv"), p.ctr("ss"), p.ctr("sm")
    stz = [p.sem(f"stz{i}") for i in range(4)]
    stu = [p.sem(f"stu{i}") for i in range(2)]
    slu = [p.sem(f"slu{i}") for i in range(2)]
    sdt = [p.sem(f"sdt{i}") for i in range(2)]
    p4 = p.sem("p4"); c4 = p.ctr("p4")
    p5 = p.sem("p5"); c5 = p.ctr("p5")

    blk = p.begin()

    @blk.vector
    def _(vector):
        vector.memset(ones_k[:], 1.0)
        vector.memset(ones_1[:], 1.0).then_inc(sv, 1)
    ones_done = cv.inc()

    @blk.gpsimd
    def _(gpsimd):
        gpsimd.dma_start(out=wxs[:], in_=w_x).then_inc(sg, 16)
        gpsimd.dma_start(out=wdts[:], in_=w_dt).then_inc(sg, 16)
    wx_done = cg.inc(32)

    @blk.sync
    def _(sync):
        sync.dma_start(out=cwsb[:], in_=cw).then_inc(ld, 16)
        sync.dma_start(out=cbsb[:], in_=cb).then_inc(ld, 16)
        sync.dma_start(out=bdtsb[:], in_=bdt).then_inc(ld, 16)
    cl.inc(48)

    prev_ln = ones_done
    ln_chunks = [(0, 512, 0)] + [(3 + t * 512, 512, 3 + t * 512) for t in range(4)]
    for (src0, w, dst0) in ln_chunks:
        @blk.sync
        def _(sync, src0=src0, w=w, prev_ln=prev_ln):
            sync.wait_ge(sv, prev_ln)
            sync.dma_start(out=xch[:, :, 0:w],
                           in_=xh[:, src0:src0 + w].rearrange(
                               "(k p) w -> p k w", p=128)).then_inc(ld, 16)
        xld = cl.inc(16)
        prev_ln = emit_ln(
            p, lambda k, w=w: xch[:, k, 0:w], NKD, w,
            lambda k, dst0=dst0, w=w: xnb[:, k, dst0:dst0 + w],
            sqb, stat_ps, bc_ps, stat_sb, ones_k, ones_1, D,
            waits_pre=[(ld, xld), (sv, prev_ln)])
    xnb_ready = prev_ln

    xs_sl = [(0, 512), (512, 1024), (1024, 1536), (1536, 2048), (1539, 2051)]
    z_sl = [(3, 515), (515, 1027), (1027, 1539), (1539, 2051)]
    ev = {}
    convd = {}
    silud = {}
    pev = p.sem("pev"); cpe = p.ctr("pev")
    gidx = 0
    pevd = {}

    for o in range(64):
        sl = xs_sl if o < 32 else z_sl

        @blk.gpsimd
        def _(gpsimd, o=o):
            if o >= 2:
                gpsimd.wait_ge(ss, ev[o - 2])
            gpsimd.dma_start(out=wsl[o % 2][:].rearrange("p (k m) -> p k m", m=128),
                             in_=w_in[o]).then_inc(wl[o % 2], 16)

        for gi in range(2):
            js = (0, 1) if gi == 0 else (2, 3)
            bset = gidx % 3

            @blk.tensor
            def _(tensor, o=o, gi=gi, js=js, bset=bset, gidx=gidx, sl=sl):
                if o == 0 and gi == 0:
                    tensor.wait_ge(sv, xnb_ready)
                if gi == 0:
                    tensor.wait_ge(wl[o % 2], 16 * (o // 2 + 1))
                if gidx >= 3:
                    psem, pval = pevd[gidx - 3]
                    tensor.wait_ge(p.sems[psem], pval)
                ins = None
                for k in range(NKD):
                    for jj, j in enumerate(js):
                        c0, c1 = sl[j]
                        ins = tensor.matmul(pj[2 * bset + jj][:, 0:c1 - c0],
                                            wsl[o % 2][:, k * 128:(k + 1) * 128],
                                            xnb[:, k, c0:c1],
                                            start=(k == 0), stop=(k == NKD - 1))
                ins.then_inc(sm, 1)
            mmd = cm.inc()

            @blk.scalar
            def _(scalar, o=o, gi=gi, js=js, bset=bset, mmd=mmd, sl=sl):
                scalar.wait_ge(sm, mmd)
                ins = None
                if o < 32:
                    if o >= 2 and gi == 0:
                        scalar.wait_ge(ss, silud[(o - 2, 3)])
                        if dbg:
                            scalar.wait_ge(p.sems[f"sxd{o % 2}"], 16 * (o // 2))
                    for jj, j in enumerate(js):
                        c0, c1 = sl[j]
                        ins = scalar.copy(xst[o % 2][:, c0:c1],
                                          pj[2 * bset + jj][:, 0:c1 - c0])
                else:
                    oz = o - 32
                    for jj, j in enumerate(js):
                        c0, c1 = sl[j]
                        if oz >= 1:
                            scalar.wait_ge(stz[j], 16 * oz)
                        ins = scalar.copy(zst[j][:, :], pj[2 * bset + jj][:, 0:c1 - c0])
                if o >= 32 and gi == 1:
                    ins.then_inc(ss, 1)
                else:
                    ins.then_inc(pev, 1)
            if o >= 32 and gi == 1:
                pevd[gidx] = ("ss", cs.inc())
                ev[o] = pevd[gidx][1]
            else:
                pevd[gidx] = ("pev", cpe.inc())
            gidx += 1

        if o < 32:
            @blk.tensor
            def _(tensor, o=o):
                if o >= 2:
                    tensor.wait_ge(ss, ev[o - 2])
                hb = (o % 2) * 512
                ins = None
                for k in range(NKD):
                    ins = tensor.matmul(bc_ps[:, hb:hb + 512],
                                        wsl[o % 2][:, k * 128:(k + 1) * 128],
                                        xnb[:, k, 1539:2051],
                                        start=(k == 0), stop=(k == NKD - 1))
                ins.then_inc(sm, 1)
            hmmd = cm.inc()

            @blk.scalar
            def _(scalar, o=o, hmmd=hmmd):
                scalar.wait_ge(sm, hmmd)
                psem, pval = pevd[2 * o + 1]
                scalar.wait_ge(p.sems[psem], pval)
                hb = (o % 2) * 512
                scalar.copy(xst[o % 2][:, 1539:2051],
                            bc_ps[:, hb:hb + 512]).then_inc(ss, 1)
            ev[o] = cs.inc()

            if dbg:
                sxd = [p.sem("sxd0"), p.sem("sxd1")]

                @blk.sync
                def _(sync, o=o):
                    sync.wait_ge(ss, ev[o])
                    sync.dma_start(out=xs_o[o * 128:(o + 1) * 128, :],
                                   in_=xst[o % 2][:]).then_inc(sxd[o % 2], 16)

            for tc in range(4):
                @blk.vector
                def _(vector, o=o, tc=tc):
                    if tc == 0:
                        vector.wait_ge(ss, ev[o])
                    if (o, tc) not in [(0, 0), (0, 1)]:
                        prev = (o, tc - 2) if tc >= 2 else (o - 1, tc + 2)
                        if prev in silud:
                            vector.wait_ge(ss, silud[prev])
                    s = xst[o % 2]
                    c0 = tc * 512
                    vector.tensor_scalar(cacc[0][:], s[:, c0:c0 + 512],
                                         cwsb[:, o, 0:1], None, ALU.mult)
                    vector.scalar_tensor_tensor(cacc[1][:], s[:, c0 + 1:c0 + 513],
                                                cwsb[:, o, 1:2], cacc[0][:],
                                                ALU.mult, ALU.add)
                    vector.scalar_tensor_tensor(cacc[2][:], s[:, c0 + 2:c0 + 514],
                                                cwsb[:, o, 2:3], cacc[1][:],
                                                ALU.mult, ALU.add)
                    vector.scalar_tensor_tensor(cfin[tc % 2][:], s[:, c0 + 3:c0 + 515],
                                                cwsb[:, o, 3:4], cacc[2][:],
                                                ALU.mult, ALU.add).then_inc(sv, 1)
                convd[(o, tc)] = cv.inc()

                @blk.scalar
                def _(scalar, o=o, tc=tc):
                    scalar.wait_ge(sv, convd[(o, tc)])
                    if o >= 2 and tc == 0:
                        scalar.wait_ge(stu[o % 2], 16 * (o // 2))
                    scalar.activation(ub[o % 2][:, tc * 512:(tc + 1) * 512],
                                      cfin[tc % 2][:], AF.Silu,
                                      bias=cbsb[:, o:o + 1]).then_inc(ss, 1)
                silud[(o, tc)] = cs.inc()

            @blk.gpsimd
            def _(gpsimd, o=o):
                gpsimd.wait_ge(ss, silud[(o, 3)])
                gpsimd.dma_start(out=u_o[o * 128:(o + 1) * 128, :],
                                 in_=ub[o % 2][:]).then_inc(stu[o % 2], 16)
        else:
            @blk.sync
            def _(sync, o=o):
                sync.wait_ge(ss, ev[o])
                oz = o - 32
                for j in range(4):
                    sync.dma_start(out=z_o[oz * 128:(oz + 1) * 128,
                                           j * 512:(j + 1) * 512],
                                   in_=zst[j][:]).then_inc(stz[j], 16)

    @blk.sync
    def _(sync):
        sync.wait_ge(stu[0], 16 * 16)
        sync.wait_ge(stu[1], 16 * 16)
        for k in range(2):
            sync.dma_start(out=uld[k][:], in_=u_o[k * 128:(k + 1) * 128, :]
                           ).then_inc(slu[k], 16)

    @blk.tensor
    def _(tensor):
        tensor.wait_ge(ss, ev[63])
        tensor.wait_ge(sg, wx_done)
        for k in range(32):
            tensor.wait_ge(slu[k % 2], 16 * (k // 2 + 1))
            ins = None
            for tc in range(4):
                tensor.matmul(pj[tc][:, :], wxs[:, k, 0:128],
                              uld[k % 2][:, tc * 512:(tc + 1) * 512],
                              start=(k == 0), stop=(k == 31))
            bc_tgts = [pj[4][0:32, :], pj[5][0:32, :],
                       bc_ps[0:32, 0:512], bc_ps[0:32, 512:1024]]
            for tc in range(4):
                ins = tensor.matmul(bc_tgts[tc], wxs[:, k, 128:160],
                                    uld[k % 2][:, tc * 512:(tc + 1) * 512],
                                    start=(k == 0), stop=(k == 31))
            ins.then_inc(p4, 1)
    c4.inc(32)

    @blk.sync
    def _(sync):
        for k in range(2, 32):
            sync.wait_ge(p4, k - 1)
            sync.dma_start(out=uld[k % 2][:], in_=u_o[k * 128:(k + 1) * 128, :]
                           ).then_inc(slu[k % 2], 16)

    @blk.scalar
    def _(scalar):
        scalar.wait_ge(p4, 32)
        ins = None
        for tc in range(4):
            ins = scalar.copy(dtrb[:, tc * 512:(tc + 1) * 512], pj[tc][:, :])
        ins.then_inc(ss, 1)
    dtr_done = cs.inc()

    for half in range(2):
        @blk.vector
        def _(vector, half=half):
            if half == 0:
                vector.wait_ge(p4, 32)
            else:
                vector.wait_ge(stz[0], 16 * 33)
                vector.wait_ge(stz[1], 16 * 33)
            srcs = [pj[4][0:32, :], pj[5][0:32, :],
                    bc_ps[0:32, 0:512], bc_ps[0:32, 512:1024]]
            ins = None
            for j in range(2):
                ins = vector.tensor_copy(bcst[j][:, :], srcs[half * 2 + j])
            ins.then_inc(sv, 1)
        bcd = cv.inc()

        @blk.sync
        def _(sync, half=half, bcd=bcd):
            sync.wait_ge(sv, bcd)
            for j in range(2):
                c0 = (half * 2 + j) * 512
                sync.dma_start(out=bc_o[:, c0:c0 + 512], in_=bcst[j][:]
                               ).then_inc(stz[j], 16)

    spd = {}
    g = 0
    for o in range(32):
        for tc in range(4):
            bank = g % 6

            @blk.tensor
            def _(tensor, o=o, tc=tc, bank=bank, g=g):
                if g == 0:
                    tensor.wait_ge(ss, dtr_done)
                    tensor.wait_ge(stz[0], 16 * 34)
                    tensor.wait_ge(stz[1], 16 * 34)
                if g >= 6:
                    tensor.wait_ge(p5, g - 5)
                tensor.matmul(pj[bank][:, :], wdts[:, o, :],
                              dtrb[:, tc * 512:(tc + 1) * 512],
                              start=True, stop=True).then_inc(sm, 1)
            mmd = cm.inc()

            @blk.scalar
            def _(scalar, o=o, tc=tc, bank=bank, g=g, mmd=mmd):
                scalar.wait_ge(sm, mmd)
                if g >= 1:
                    scalar.wait_ge(sv, spd[g - 1])
                b4 = (g % 2) * 4
                scalar.activation(sp[b4 + 0][:], pj[bank][:, :], AF.Relu,
                                  bias=bdtsb[:, o:o + 1])
                scalar.activation(sp[b4 + 1][:], pj[bank][:, :], AF.Abs,
                                  bias=bdtsb[:, o:o + 1]).then_inc(p5, 1)
                scalar.activation(sp[b4 + 2][:], sp[b4 + 1][:], AF.Exp, scale=-1.0)
                scalar.activation(sp[b4 + 3][:], sp[b4 + 2][:], AF.Ln,
                                  bias=1.0).then_inc(ss, 1)
            c5.inc()
            sfd = cs.inc()

            @blk.vector
            def _(vector, g=g, sfd=sfd):
                vector.wait_ge(ss, sfd)
                b4 = (g % 2) * 4
                if g >= 2:
                    vector.wait_ge(sdt[g % 2], 16 * (g // 2))
                vector.tensor_add(dtst[g % 2][:, :], sp[b4 + 0][:], sp[b4 + 3][:]
                                  ).then_inc(sv, 1)
            spd[g] = cv.inc()

            @blk.sync
            def _(sync, o=o, tc=tc, g=g):
                sync.wait_ge(sv, spd[g])
                sync.dma_start(out=dt_o[o * 128:(o + 1) * 128,
                                        tc * 512:(tc + 1) * 512],
                               in_=dtst[g % 2][:]).then_inc(sdt[g % 2], 16)
            g += 1

    @blk.sync
    def _(sync):
        sync.wait_ge(sdt[0], 16 * 64)
        sync.wait_ge(sdt[1], 16 * 64)
        sync.wait_ge(stz[0], 16 * 34)
        sync.wait_ge(stz[1], 16 * 34)
        sync.wait_ge(stz[2], 16 * 32)
        sync.wait_ge(stz[3], 16 * 32)
        sync.wait_ge(stu[0], 16 * 16)
        sync.wait_ge(stu[1], 16 * 16)
    return p


def stage_l0a(ins, core):
    b, th = core // 2, core % 2
    x = ins["L0_attn_x"][b]
    xT = np.ascontiguousarray(x.T)
    s = th * TOK
    if th == 0:
        xhv = np.concatenate([np.zeros((D, 3), np.float32), xT[:, 0:TOK]], axis=1)
    else:
        xhv = xT[:, s - 3:s + TOK]
    return {
        "xh": np.ascontiguousarray(xhv),
        "w_in": tile_w(ins["L0_attn_W_in"]),
        "w_x": np.ascontiguousarray(
            ins["L0_attn_W_x"].T.reshape(32, 128, 160).transpose(1, 0, 2)),
        "w_dt": np.ascontiguousarray(ins["L0_attn_W_dt"].T.reshape(128, 32, 128)),
        "cw": np.ascontiguousarray(
            ins["L0_attn_conv_w"].reshape(32, 128, 4).transpose(1, 0, 2)),
        "cb": tile_vec(ins["L0_attn_conv_b"]),
        "bdt": tile_vec(ins["L0_attn_b_dt"]),
    }



def build_l0scan():
    p = Prog("l0scan")
    nc = p.nc
    NT = 8
    NK = 16
    NS = 16
    u_i = nc.dram_tensor("u_i", [2048, L], BF16, kind="ExternalInput").ap()
    dt_i = nc.dram_tensor("dt_i", [2048, L], BF16, kind="ExternalInput").ap()
    z_i = nc.dram_tensor("z_i", [2048, L], F32, kind="ExternalInput").ap()
    bc_i = nc.dram_tensor("bc_i", [32, L], F32, kind="ExternalInput").ap()
    alog_i = nc.dram_tensor("alog_i", [128, NK, NS], F32, kind="ExternalInput").ap()
    dvec_i = nc.dram_tensor("dvec_i", [128, NK], F32, kind="ExternalInput").ap()
    ident_i = nc.dram_tensor("ident_i", [128, 128], F32, kind="ExternalInput").ap()
    yg_o = nc.dram_tensor("yg_o", [2048, L], F32, kind="ExternalOutput").ap()

    W = 512
    dtch = p.sbuf("dtch", [128, NK, W], BF16)
    uch = p.sbuf("uch", [128, NK, W], BF16)
    zch = p.sbuf("zch", [128, NK, W], BF16)
    Bbc = p.sbuf("Bbc", [128, NS, W], BF16)
    Cbc = p.sbuf("Cbc", [128, NS, W], BF16)
    alog_sb = p.sbuf("alog_sb", [128, NK, NS], F32)
    A_sb = p.sbuf("A_sb", [128, NK, NS], F32)
    eA_sb = p.sbuf("eA_sb", [128, NK, NS], F32)
    dvec_sb = p.sbuf("dvec_sb", [128, NK], F32)
    idb = p.sbuf("idb", [128, 128], BF16)
    dtu = p.sbuf("dtu", [128, W], BF16)
    dA_sl = [p.sbuf(f"dA{i}", [128, W], BF16) for i in range(2)]
    dBu_sl = [p.sbuf(f"dBu{i}", [128, W], BF16) for i in range(2)]
    h_all = p.sbuf("h_all", [128, NS, W], BF16)
    hc_sl = [p.sbuf(f"hc{i}", [128, W], BF16) for i in range(2)]
    hlast = p.sbuf("hlast", [128, NK * NS], F32)
    ysb = [p.sbuf(f"ysb{i}", [128, W], F32) for i in range(2)]
    ud_sb = p.sbuf("ud_sb", [128, W], F32)
    yd_sb = p.sbuf("yd_sb", [128, W], F32)
    sz_sb = p.sbuf("sz_sb", [128, W], F32)
    ygs = [p.sbuf(f"ygs{i}", [128, W], F32) for i in range(2)]

    yps = [p.psum(f"yps{i}", [128, 512], F32) for i in range(4)]

    ld = p.sem("ld"); cl = p.ctr("ld")
    gb = p.sem("gb"); cgb = p.ctr("gb")
    sv, ss, sm = p.sem("sv"), p.sem("ss"), p.sem("sm")
    cv, cs, cm = p.ctr("sv"), p.ctr("ss"), p.ctr("sm")
    pev = p.sem("pev"); cpe = p.ctr("pev")
    sty = [p.sem(f"sty{i}") for i in range(2)]

    blk = p.begin()

    @blk.sync
    def _(sync):
        sync.dma_start(out=alog_sb[:], in_=alog_i).then_inc(ld, 16)
        sync.dma_start(out=dvec_sb[:], in_=dvec_i).then_inc(ld, 16)
    cl.inc(32)

    @blk.gpsimd
    def _(gpsimd):
        gpsimd.dma_start(out=idb[:], in_=ident_i).then_inc(gb, 16)
    cgb.inc(16)

    @blk.vector
    def _(vector):
        vector.memset(hlast[:], 0.0).then_inc(sv, 1)
    hl0 = cv.inc()

    @blk.scalar
    def _(scalar):
        scalar.wait_ge(ld, 32)
        scalar.activation(eA_sb[:], alog_sb[:], AF.Exp).then_inc(ss, 1)
    eA_done = cs.inc()

    @blk.vector
    def _(vector):
        vector.wait_ge(ss, eA_done)
        vector.tensor_scalar(A_sb[:], eA_sb[:], -1.0, None, ALU.mult).then_inc(sv, 1)
    A_done = cv.inc()

    dAd = {}
    scand = {}
    hcd = {}
    mmd = {}
    evd = {}
    szd = {}
    ygd = {}
    gk = 0
    for tc in range(NT):
        c0 = tc * W
        if tc == 0:
            wait_prev = [(sv, A_done)]
        else:
            wait_prev = [(sv, ygd[(tc - 1, NK - 1)]), (ss, szd[(tc - 1, NK - 1)])]

        @blk.sync
        def _(sync, c0=c0, wait_prev=wait_prev):
            _waits(sync, wait_prev)
            sync.dma_start(out=dtch[:], in_=dt_i[:, c0:c0 + W].rearrange(
                "(k p) w -> p k w", p=128)).then_inc(ld, 16)
            sync.dma_start(out=uch[:], in_=u_i[:, c0:c0 + W].rearrange(
                "(k p) w -> p k w", p=128)).then_inc(ld, 16)
        lds = cl.inc(32)

        @blk.gpsimd
        def _(gpsimd, c0=c0, wait_prev=wait_prev):
            _waits(gpsimd, wait_prev)
            gpsimd.dma_start(out=zch[:], in_=z_i[:, c0:c0 + W].rearrange(
                "(k p) w -> p k w", p=128)).then_inc(gb, 16)
            gpsimd.dma_start(
                out=Bbc[:],
                in_=bc_i[0:16, c0:c0 + W].unsqueeze(0).broadcast_to(
                    [128, 16, W])).then_inc(gb, 16)
            gpsimd.dma_start(
                out=Cbc[:],
                in_=bc_i[16:32, c0:c0 + W].unsqueeze(0).broadcast_to(
                    [128, 16, W])).then_inc(gb, 16)
        gbs = cgb.inc(48)

        for k in range(NK):
            @blk.vector
            def _(vector, tc=tc, k=k, lds=lds, gbs=gbs):
                if k == 0:
                    vector.wait_ge(ld, lds)
                    vector.wait_ge(gb, gbs)
                vector.tensor_tensor(dtu[:], dtch[:, k, :], uch[:, k, :],
                                     ALU.mult).then_inc(sv, 1)
            dtu_done = cv.inc()

            for s in range(NS):
                @blk.scalar
                def _(scalar, tc=tc, k=k, s=s, lds=lds, gbs=gbs):
                    g = (tc * NK + k) * NS + s
                    if g >= 2:
                        scalar.wait_ge(sv, scand[g - 2])
                    if g == 0:
                        scalar.wait_ge(sv, A_done)
                    if k == 0 and s == 0:
                        scalar.wait_ge(ld, lds)
                        scalar.wait_ge(gb, gbs)
                    scalar.activation(dA_sl[s % 2][:], dtch[:, k, :], AF.Exp,
                                      scale=A_sb[:, k, s:s + 1]).then_inc(ss, 1)
                dAd[(tc, k, s)] = cs.inc()

                @blk.vector
                def _(vector, tc=tc, k=k, s=s, dtu_done=dtu_done):
                    g = (tc * NK + k) * NS + s
                    vector.tensor_tensor(dBu_sl[s % 2][:], dtu[:], Bbc[:, s, :],
                                         ALU.mult)
                    vector.wait_ge(ss, dAd[(tc, k, s)])
                    vector.tensor_tensor_scan(
                        h_all[:, s, :], dA_sl[s % 2][:], dBu_sl[s % 2][:],
                        hlast[:, k * NS + s:k * NS + s + 1],
                        ALU.mult, ALU.add).then_inc(sv, 1)
                scand[(tc * NK + k) * NS + s] = cv.inc()

                @blk.vector
                def _(vector, tc=tc, k=k, s=s):
                    if gk * NS + s >= 2:
                        pg = (tc * NK + k) * NS + s - 2
                        if pg in mmd:
                            vector.wait_ge(sm, mmd[pg])
                    vector.tensor_tensor(hc_sl[s % 2][:], h_all[:, s, :],
                                         Cbc[:, s, :], ALU.mult).then_inc(sv, 1)
                hcd[(tc, k, s)] = cv.inc()

                @blk.tensor
                def _(tensor, tc=tc, k=k, s=s, gk=gk):
                    tensor.wait_ge(sv, hcd[(tc, k, s)])
                    if s == 0:
                        tensor.wait_ge(gb, 16)
                        if gk >= 4:
                            tensor.wait_ge(pev, gk - 3)
                    tensor.matmul(yps[gk % 4][:, :], idb[:], hc_sl[s % 2][:],
                                  start=(s == 0), stop=(s == NS - 1)).then_inc(sm, 1)
                mmd[(tc * NK + k) * NS + s] = cm.inc()

            @blk.vector
            def _(vector, tc=tc, k=k):
                vector.tensor_copy(hlast[:, k * NS:(k + 1) * NS],
                                   h_all[:, :, W - 1:W].rearrange("p s w -> p (s w)")
                                   ).then_inc(sv, 1)
            cv.inc()

            @blk.scalar
            def _(scalar, tc=tc, k=k, gk=gk):
                scalar.wait_ge(sm, mmd[(tc * NK + k) * NS + NS - 1])
                if gk >= 1:
                    pk = (tc, k - 1) if k >= 1 else (tc - 1, NK - 1)
                    scalar.wait_ge(sv, ygd[pk])
                scalar.copy(ysb[k % 2][:], yps[gk % 4][:, :]).then_inc(pev, 1)
                scalar.activation(sz_sb[:], zch[:, k, :], AF.Silu).then_inc(ss, 1)
            cpe.inc()
            szd[(tc, k)] = cs.inc()

            @blk.vector
            def _(vector, tc=tc, k=k):
                vector.tensor_scalar(ud_sb[:], uch[:, k, :], dvec_sb[:, k:k + 1],
                                     None, ALU.mult)
                vector.wait_ge(ss, szd[(tc, k)])
                vector.tensor_add(yd_sb[:], ysb[k % 2][:], ud_sb[:])
                if gk >= 2:
                    vector.wait_ge(sty[k % 2], 16 * (gk // 2))
                vector.tensor_tensor(ygs[k % 2][:], yd_sb[:], sz_sb[:],
                                     ALU.mult).then_inc(sv, 1)
            ygd[(tc, k)] = cv.inc()

            @blk.sync
            def _(sync, tc=tc, k=k, c0=c0):
                sync.wait_ge(sv, ygd[(tc, k)])
                sync.dma_start(out=yg_o[k * 128:(k + 1) * 128, c0:c0 + W],
                               in_=ygs[k % 2][:]).then_inc(sty[k % 2], 16)
            gk += 1

    @blk.sync
    def _(sync):
        sync.wait_ge(sty[0], 16 * 64)
        sync.wait_ge(sty[1], 16 * 64)
    return p


def stage_l0scan(outs_a, ins, core):
    b, dh = core // 2, core % 2
    r0, r1 = outs_a[2 * b], outs_a[2 * b + 1]
    sl = slice(dh * 2048, (dh + 1) * 2048)
    u = np.concatenate([r0["u_o"], r1["u_o"]], axis=1)[sl]
    dt = np.concatenate([r0["dt_o"], r1["dt_o"]], axis=1)[sl]
    z = np.concatenate([r0["z_o"], r1["z_o"]], axis=1)[sl]
    bc = np.concatenate([r0["bc_o"], r1["bc_o"]], axis=1)
    alog = ins["L0_attn_A_log"][sl]
    alog_t = np.ascontiguousarray(alog.reshape(16, 128, 16).transpose(1, 0, 2))
    dvec = tile_vec(ins["L0_attn_D"][sl])
    return {
        "u_i": np.ascontiguousarray(u),
        "dt_i": np.ascontiguousarray(dt),
        "z_i": np.ascontiguousarray(z),
        "bc_i": np.ascontiguousarray(bc),
        "alog_i": alog_t,
        "dvec_i": dvec,
        "ident_i": np.eye(128, dtype=np.float32),
    }



def emit_mm_stream(p, name, w_dram, n_o, nk, slices, rhs_view, evac_fn,
                   banks, wsl, pre_wait=(), nslot=2, ev_name=None):
    blk = p.block
    wl = [p.sem(f"{name}_wl{i}") for i in range(2)]
    sev = p.sem(ev_name or f"{name}_ev")
    smm = p.sem("sm"); cmm = p.ctr("sm")
    nsl = len(slices)
    nsets = len(banks) // nsl
    assert nsets >= 2
    evd = {}
    mmds = {}
    for o in range(n_o):
        @blk.gpsimd
        def _(g, o=o):
            if o < nslot:
                _waits(g, pre_wait)
            if o >= nslot:
                g.wait_ge(smm, mmds[o - nslot])
            if o >= nsets:
                g.wait_ge(sev, evd[o - nsets])
            g.dma_start(out=wsl[o % nslot][:].rearrange("p (k m) -> p k m", m=128),
                        in_=w_dram[o]).then_inc(wl[o % nslot], 16)

        @blk.tensor
        def _(t, o=o):
            if o == 0:
                _waits(t, pre_wait)
            t.wait_ge(wl[o % nslot], 16 * (o // nslot + 1))
            if o >= nsets:
                t.wait_ge(sev, evd[o - nsets])
            bs = [banks[(o % nsets) * nsl + j] for j in range(nsl)]
            ins = None
            for k in range(nk):
                for j, (c0, c1) in enumerate(slices):
                    ins = t.matmul(bs[j][:, 0:c1 - c0],
                                   wsl[o % nslot][:, k * 128:(k + 1) * 128],
                                   rhs_view(k, c0, c1),
                                   start=(k == 0), stop=(k == nk - 1))
            ins.then_inc(smm, 1)
        mmds[o] = cmm.inc()
        evd[o] = evac_fn(o, [banks[(o % nsets) * nsl + j] for j in range(nsl)],
                         mmds[o])
    return evd


def build_l0out():
    p = Prog("l0out")
    nc = p.nc
    yg_i = nc.dram_tensor("yg_i", [4096, TOK], F32, kind="ExternalInput").ap()
    w_out = nc.dram_tensor("w_out", [16, 128, 32, 128], F32, kind="ExternalInput").ap()
    x_i = nc.dram_tensor("x_i", [D, TOK], F32, kind="ExternalInput").ap()
    x1_o = nc.dram_tensor("x1_o", [D, TOK], F32, kind="ExternalOutput").ap()

    ygb = p.sbuf("ygb", [128, 32, TOK], BF16)
    wsl = [p.sbuf(f"wsl{i}", [128, 32 * 128], BF16) for i in range(2)]
    xres = [p.sbuf(f"xres{i}", [128, TOK], F32) for i in range(2)]
    outs = [p.sbuf(f"outs{i}", [128, TOK], F32) for i in range(2)]
    banks = [p.psum(f"pb{i}", [128, 512], F32) for i in range(8)]

    gb = p.sem("gb"); cgb = p.ctr("gb")
    lx = [p.sem(f"lx{i}") for i in range(2)]
    sv = p.sem("sv"); cv = p.ctr("sv")
    sto = [p.sem(f"sto{i}") for i in range(2)]
    sev = p.sem("c_ev"); cev = p.ctr("c_ev")

    blk = p.begin()

    @blk.gpsimd
    def _(g):
        g.dma_start(out=ygb[:], in_=yg_i.rearrange("(k p) w -> p k w", p=128)
                    ).then_inc(gb, 16)
    rhs_rdy = cgb.inc(16)

    evds = {}

    def evac(o, bs, mmd):
        @blk.sync
        def _(sync, o=o):
            if o >= 2:
                sync.wait_ge(sev, evds[o - 2])
            sync.dma_start(out=xres[o % 2][:],
                           in_=x_i[o * 128:(o + 1) * 128, :]).then_inc(lx[o % 2], 16)

        @blk.vector
        def _(vector, o=o, bs=bs, mmd=mmd):
            vector.wait_ge(p.sems["sm"], mmd)
            vector.wait_ge(lx[o % 2], 16 * (o // 2 + 1))
            if o >= 2:
                vector.wait_ge(sto[o % 2], 16 * (o // 2))
            ins = None
            for j in range(4):
                c0 = j * 512
                ins = vector.tensor_add(outs[o % 2][:, c0:c0 + 512],
                                        bs[j][:, :], xres[o % 2][:, c0:c0 + 512])
            ins.then_inc(sev, 1)
        evds[o] = cev.inc()

        @blk.sync
        def _(sync, o=o):
            sync.wait_ge(sev, evds[o])
            sync.dma_start(out=x1_o[o * 128:(o + 1) * 128, :],
                           in_=outs[o % 2][:]).then_inc(sto[o % 2], 16)
        return evds[o]

    emit_mm_stream(p, "c", w_out, 16, 32,
                   [(0, 512), (512, 1024), (1024, 1536), (1536, 2048)],
                   lambda k, c0, c1: ygb[:, k, c0:c1], evac, banks, wsl,
                   pre_wait=[(gb, rhs_rdy)])

    @blk.sync
    def _(sync):
        sync.wait_ge(sto[0], 16 * 8)
        sync.wait_ge(sto[1], 16 * 8)
    return p


def stage_l0out(outs_b, ins, core):
    b, th = core // 2, core % 2
    r0, r1 = outs_b[2 * b], outs_b[2 * b + 1]
    yg = np.concatenate([r0["yg_o"], r1["yg_o"]], axis=0)
    s = th * TOK
    x = ins["L0_attn_x"][b]
    return {
        "yg_i": np.ascontiguousarray(yg[:, s:s + TOK]),
        "w_out": tile_w(ins["L0_attn_W_out"]),
        "x_i": np.ascontiguousarray(x[s:s + TOK].T),
    }



def build_l0ffn():
    p = Prog("l0ffn")
    nc = p.nc
    x1_i = nc.dram_tensor("x1_i", [D, TOK], F32, kind="ExternalInput").ap()
    w1_t = nc.dram_tensor("w1_t", [64, 128, NKD, 128], F32, kind="ExternalInput").ap()
    w2_t = nc.dram_tensor("w2_t", [16, 128, 64, 128], F32, kind="ExternalInput").ap()
    b1_i = nc.dram_tensor("b1_i", [128, 64], F32, kind="ExternalInput").ap()
    b2_i = nc.dram_tensor("b2_i", [128, 16], F32, kind="ExternalInput").ap()
    x2_o = nc.dram_tensor("x2_o", [D, TOK], F32, kind="ExternalOutput").ap()
    wq1_s = nc.dram_tensor("wq1_s", [64, 128, NKD, 128], BF16, kind="Internal").ap()
    wq2_s = nc.dram_tensor("wq2_s", [16, 128, 64, 128], BF16, kind="Internal").ap()

    xch = p.sbuf("xch", [128, NKD, 512], F32)
    sqb = p.sbuf("sqb", [128, 2 * 512], F32)
    xnq = p.sbuf("xnq", [128, NKD, 512], BF16)
    stat_sb = p.sbuf("stat", [1, 6 * 512], F32)
    ones_k = p.sbuf("ones_k", [128, 1], F32)
    ones_1 = p.sbuf("ones_1", [1, 128], F32)
    hq = p.sbuf("hq", [128, 64, 512], BF16)
    scsl = [p.sbuf(f"scsl{i}", [128, 2048], F32) for i in range(2)]
    absac = p.sbuf("absac", [128, 1], F32)
    accs = [p.sbuf(f"acc{i}", [128, 1], F32) for i in range(4)]
    thr1 = p.sbuf("thr1", [128, 1], F32)
    thr2 = p.sbuf("thr2", [128, 1], F32)
    tsc = p.sbuf("tsc", [1, 2], F32)
    qs = p.sbuf("qs", [128, 2048], BF16)
    qa = p.sbuf("qa", [128, 2048], F32)
    qm = p.sbuf("qm", [128, 2048], BF16)
    wqs = [p.sbuf(f"wqs{i}", [128, 2048], BF16) for i in range(2)]
    wsl = [p.sbuf(f"wsl{i}", [128, 2048], BF16) for i in range(2)]
    wsl2 = [p.sbuf("wsl2_0", [128, 8192], BF16)]
    b1s = p.sbuf("b1s", [128, 64], F32)
    b2s = p.sbuf("b2s", [128, 16], F32)
    xres = [p.sbuf(f"xres{i}", [128, 512], F32) for i in range(2)]
    outs = [p.sbuf(f"outs{i}", [128, 512], F32) for i in range(2)]
    t1sb = p.sbuf("t1sb", [128, 512], F32)
    banks = [p.psum(f"pb{i}", [128, 512], F32) for i in range(6)]
    bc_ps = p.psum("bc_ps", [128, 1024], F32)

    ld = p.sem("ld"); cl = p.ctr("ld")
    sv, ss, sm = p.sem("sv"), p.sem("ss"), p.sem("sm")
    cv, cs, cm = p.ctr("sv"), p.ctr("ss"), p.ctr("sm")
    lsw = [p.sem(f"lsw{i}") for i in range(2)]
    sqw = [p.sem(f"sqw{i}") for i in range(2)]
    lx = [p.sem(f"lx{i}") for i in range(2)]
    sto = [p.sem(f"sto{i}") for i in range(2)]
    sg = p.sem("sg"); cg = p.ctr("sg")

    blk = p.begin()

    @blk.vector
    def _(vector):
        vector.memset(ones_k[:], 1.0)
        vector.memset(ones_1[:], 1.0)
        vector.memset(accs[0][:], 0.0)
        vector.memset(accs[2][:], 0.0).then_inc(sv, 1)
    ones_done = cv.inc()

    @blk.sync
    def _(sync):
        sync.dma_start(out=b1s[:], in_=b1_i).then_inc(ld, 16)
        sync.dma_start(out=b2s[:], in_=b2_i).then_inc(ld, 16)
    cl.inc(32)

    units = [("w1", i) for i in range(64)] + [("w2", i) for i in range(64)]
    absd = {}
    accd = {}
    for i, (mat, idx) in enumerate(units):
        @blk.sync
        def _(sync, i=i, mat=mat, idx=idx):
            if i >= 2:
                sync.wait_ge(ss, absd[i - 2])
            if mat == "w1":
                srcv = w1_t[idx].rearrange("p a b -> p (a b)")
            else:
                srcv = w2_t[idx // 4][:, (idx % 4) * 16:(idx % 4) * 16 + 16, :].rearrange(
                    "p a b -> p (a b)")
            sync.dma_start(out=scsl[i % 2][:], in_=srcv).then_inc(lsw[i % 2], 16)

        @blk.scalar
        def _(scalar, i=i):
            scalar.wait_ge(lsw[i % 2], 16 * (i // 2 + 1))
            if i >= 1:
                scalar.wait_ge(sv, accd[i - 1])
            scalar.activation(qa[:], scsl[i % 2][:], AF.Abs,
                              accum_out=absac[:]).then_inc(ss, 1)
        absd[i] = cs.inc()

        @blk.vector
        def _(vector, i=i, mat=mat):
            vector.wait_ge(ss, absd[i])
            a0 = 0 if mat == "w1" else 2
            par = (i % 2) if mat == "w1" else (i % 2)
            vector.tensor_add(accs[a0 + 1 - par][:], accs[a0 + par][:],
                              absac[:]).then_inc(sv, 1)
        accd[i] = cv.inc()
    w1_fin = accs[0 + 1 - (63 % 2)]
    w2_fin = accs[2 + 1 - (127 % 2)]

    @blk.tensor
    def _(tensor):
        tensor.wait_ge(sv, accd[127])
        tensor.matmul(bc_ps[0:1, 0:1], w1_fin[:], ones_k[:, 0:1],
                      start=True, stop=True)
        tensor.matmul(bc_ps[0:1, 512:513], w2_fin[:], ones_k[:, 0:1],
                      start=True, stop=True).then_inc(sm, 1)
    sums_done = cm.inc()

    @blk.vector
    def _(vector):
        vector.wait_ge(sm, sums_done)
        vector.tensor_scalar(tsc[0:1, 0:1], bc_ps[0:1, 0:1],
                             0.7 / (8192.0 * 2048.0), None, ALU.mult)
        vector.tensor_scalar(tsc[0:1, 1:2], bc_ps[0:1, 512:513],
                             0.7 / (8192.0 * 2048.0), None, ALU.mult).then_inc(sv, 1)
    thr_done = cv.inc()

    @blk.tensor
    def _(tensor):
        tensor.wait_ge(sv, thr_done)
        tensor.matmul(bc_ps[:, 0:1], ones_1[0:1, :], tsc[0:1, 0:1],
                      start=True, stop=True)
        tensor.matmul(bc_ps[:, 512:513], ones_1[0:1, :], tsc[0:1, 1:2],
                      start=True, stop=True).then_inc(sm, 1)
    thrb_done = cm.inc()

    @blk.scalar
    def _(scalar):
        scalar.wait_ge(sm, thrb_done)
        scalar.copy(thr1[:], bc_ps[:, 0:1])
        scalar.copy(thr2[:], bc_ps[:, 512:513]).then_inc(ss, 1)
    thr_sb = cs.inc()

    qd = {}
    qd2 = {}
    for i, (mat, idx) in enumerate(units):
        @blk.sync
        def _(sync, i=i, mat=mat, idx=idx):
            if i >= 2:
                sync.wait_ge(ss, qd[i - 2])
            if mat == "w1":
                srcv = w1_t[idx].rearrange("p a b -> p (a b)")
            else:
                srcv = w2_t[idx // 4][:, (idx % 4) * 16:(idx % 4) * 16 + 16, :].rearrange(
                    "p a b -> p (a b)")
            sync.dma_start(out=scsl[i % 2][:], in_=srcv).then_inc(lsw[i % 2], 16)

        @blk.scalar
        def _(scalar, i=i, mat=mat):
            scalar.wait_ge(lsw[i % 2], 16 * (64 + i // 2 + 1))
            if i == 0:
                scalar.wait_ge(ss, thr_sb)
            if i >= 1:
                scalar.wait_ge(sv, qd2[i - 1])
            scalar.activation(qs[:], scsl[i % 2][:], AF.Sign)
            scalar.activation(qa[:], scsl[i % 2][:], AF.Abs).then_inc(ss, 1)
        qd[i] = cs.inc()

        @blk.vector
        def _(vector, i=i, mat=mat):
            vector.wait_ge(ss, qd[i])
            thr = thr1 if mat == "w1" else thr2
            vector.tensor_scalar(qm[:], qa[:], thr[:, 0:1], None, ALU.is_ge)
            if i >= 2:
                vector.wait_ge(sqw[i % 2], 16 * (i // 2))
            vector.tensor_tensor(wqs[i % 2][:], qs[:], qm[:], ALU.mult
                                 ).then_inc(sv, 1)
        qd2[i] = cv.inc()

        @blk.gpsimd
        def _(g, i=i, mat=mat, idx=idx):
            g.wait_ge(sv, qd2[i])
            if mat == "w1":
                dstv = wq1_s[idx]
            else:
                dstv = wq2_s[idx // 4][:, (idx % 4) * 16:(idx % 4) * 16 + 16, :]
            g.dma_start(out=dstv,
                        in_=wqs[i % 2][:].rearrange("p (k m) -> p k m", m=128)
                        ).then_inc(sqw[i % 2], 16)

    prev_ln = ones_done
    evds1 = {}
    evds2 = {}
    sev1 = p.sem("d1_ev"); cev1 = p.ctr("d1_ev")
    sev2 = p.sem("d2_ev"); cev2 = p.ctr("d2_ev")
    qtr_last = {}
    for tq in range(4):
        c0 = tq * 512

        @blk.sync
        def _(sync, c0=c0, prev_ln=prev_ln, tq=tq):
            sync.wait_ge(sv, prev_ln)
            sync.dma_start(out=xch[:], in_=x1_i[:, c0:c0 + 512].rearrange(
                "(k p) w -> p k w", p=128)).then_inc(ld, 16)
        xld = cl.inc(16)
        w_pre = [(ld, xld), (sv, prev_ln)]
        if tq == 0:
            w_pre.append((ss, thr_sb))
        else:
            w_pre.append((sev2, evds2[tq * 16 - 1]))
        prev_ln = emit_ln(
            p, lambda k: xch[:, k, :], NKD, 512,
            lambda k: xnq[:, k, :],
            sqb, (banks[0], banks[1]), bc_ps, stat_sb, ones_k, ones_1, D,
            waits_pre=w_pre)

        def evac1(o, bs, mmd, tq=tq):
            @blk.scalar
            def _(scalar, o=o, bs=bs, mmd=mmd, tq=tq):
                scalar.wait_ge(p.sems["sm"], mmd)
                if o == 0 and tq == 0:
                    scalar.wait_ge(ld, 32)
                scalar.activation(hq[:, o, :], bs[0][:, :], AF.Gelu,
                                  bias=b1s[:, o:o + 1]).then_inc(sev1, 1)
            return cev1.inc()

        pre1 = [(sv, prev_ln)]
        if tq == 0:
            pre1 += [(sqw[0], 16 * 64), (sqw[1], 16 * 64)]
        emit_mm_stream(p, f"d1_{tq}", wq1_s, 64, NKD,
                       [(0, 512)], lambda k, a, b: xnq[:, k, a:b], evac1,
                       banks, wsl, pre_wait=pre1, ev_name="d1_ev")
        mm1_last = cev1.v

        def evac2(o, bs, mmd, tq=tq, c0=c0):
            g2 = tq * 16 + o

            @blk.sync
            def _(sync, o=o, g2=g2, c0=c0):
                if g2 >= 2:
                    sync.wait_ge(sev2, evds2[g2 - 2])
                sync.dma_start(out=xres[g2 % 2][:],
                               in_=x1_i[o * 128:(o + 1) * 128, c0:c0 + 512]
                               ).then_inc(lx[g2 % 2], 16)

            @blk.scalar
            def _(scalar, o=o, bs=bs, mmd=mmd, g2=g2):
                scalar.wait_ge(p.sems["sm"], mmd)
                if g2 >= 1:
                    scalar.wait_ge(sev2, evds2[g2 - 1])
                scalar.copy(t1sb[:], bs[0][:, :]).then_inc(ss, 1)
            cpd = cs.inc()

            @blk.vector
            def _(vector, o=o, g2=g2, cpd=cpd):
                vector.wait_ge(ss, cpd)
                if g2 == 0:
                    vector.wait_ge(ld, 32)
                vector.wait_ge(lx[g2 % 2], 16 * (g2 // 2 + 1))
                if g2 >= 2:
                    vector.wait_ge(sto[g2 % 2], 16 * (g2 // 2))
                vector.scalar_tensor_tensor(outs[g2 % 2][:], t1sb[:],
                                            b2s[:, o:o + 1], xres[g2 % 2][:],
                                            ALU.add, ALU.add).then_inc(sev2, 1)
            evds2[g2] = cev2.inc()

            @blk.sync
            def _(sync, o=o, g2=g2, c0=c0):
                sync.wait_ge(sev2, evds2[g2])
                sync.dma_start(out=x2_o[o * 128:(o + 1) * 128, c0:c0 + 512],
                               in_=outs[g2 % 2][:]).then_inc(sto[g2 % 2], 16)
            return evds2[g2]

        pre2 = [(sev1, mm1_last)]
        if tq == 0:
            pre2 += [(sqw[0], 16 * 64), (sqw[1], 16 * 64)]
        emit_mm_stream(p, f"d2_{tq}", wq2_s, 16, 64,
                       [(0, 512)], lambda k, a, b: hq[:, k, a:b], evac2,
                       banks, wsl2, pre_wait=pre2, nslot=1, ev_name="d2_ev")
        qtr_last[tq] = prev_ln

    @blk.sync
    def _(sync):
        sync.wait_ge(sto[0], 16 * 32)
        sync.wait_ge(sto[1], 16 * 32)
    return p


def stage_l0ffn(outs_c, ins, core):
    b, th = core // 2, core % 2
    x1 = outs_c[core]["x1_o"]
    return {
        "x1_i": np.ascontiguousarray(x1),
        "w1_t": tile_w(ins["L0_ffn_w1"]),
        "w2_t": tile_w(ins["L0_ffn_w2"]),
        "b1_i": tile_vec(ins["L0_ffn_b1"]),
        "b2_i": tile_vec(ins["L0_ffn_b2"]),
    }



def build_l1pre():
    p = Prog("l1pre")
    nc = p.nc
    HW = 2051
    xh = nc.dram_tensor("xh", [D, HW], F32, kind="ExternalInput").ap()
    w_ip = nc.dram_tensor("w_ip", [66, 128, NKD, 128], F32, kind="ExternalInput").ap()
    cw = nc.dram_tensor("cw", [128, 33, 4], F32, kind="ExternalInput").ap()
    cb = nc.dram_tensor("cb", [128, 33], F32, kind="ExternalInput").ap()
    dtb = nc.dram_tensor("dtb", [128, 1], F32, kind="ExternalInput").ap()
    z1_o = nc.dram_tensor("z1_o", [4096, TOK], F32, kind="ExternalOutput").ap()
    xsc_o = nc.dram_tensor("xsc_o", [4224, TOK], F32, kind="ExternalOutput").ap()
    dt1_o = nc.dram_tensor("dt1_o", [32, TOK], F32, kind="ExternalOutput").ap()

    xch = p.sbuf("xch", [128, NKD, 512], F32)
    sqb = p.sbuf("sqb", [128, 2 * 512], F32)
    xnb = p.sbuf("xnb", [128, NKD, HW], BF16)
    stat_sb = p.sbuf("stat", [1, 6 * 512], F32)
    ones_k = p.sbuf("ones_k", [128, 1], F32)
    ones_1 = p.sbuf("ones_1", [1, 128], F32)
    wsl = [p.sbuf(f"wsl{i}", [128, NKD * 128], BF16) for i in range(2)]
    cwsb = p.sbuf("cwsb", [128, 33, 4], F32)
    cbsb = p.sbuf("cbsb", [128, 33], F32)
    dtbsb = p.sbuf("dtbsb", [128, 1], F32)
    xst = [p.sbuf(f"xst{i}", [128, HW], F32) for i in range(2)]
    cacc = [p.sbuf(f"cacc{i}", [128, 512], F32) for i in range(3)]
    cfin = [p.sbuf(f"cfin{i}", [128, 512], F32) for i in range(2)]
    ub = [p.sbuf(f"ub{i}", [128, 2048], F32) for i in range(2)]
    zst = [p.sbuf(f"zst{i}", [128, 512], F32) for i in range(4)]
    sp = [p.sbuf(f"sp{i}", [128, 512], F32) for i in range(8)]
    dtst = [p.sbuf(f"dtst{i}", [128, 512], F32) for i in range(2)]

    bc_ps = p.psum("bc_ps", [128, 1024], F32)
    pj = [p.psum(f"pj{i}", [128, 512], F32) for i in range(6)]

    ld = p.sem("ld"); cl = p.ctr("ld")
    wl = [p.sem(f"wl{i}") for i in range(2)]
    sv, ss, sm = p.sem("sv"), p.sem("ss"), p.sem("sm")
    cv, cs, cm = p.ctr("sv"), p.ctr("ss"), p.ctr("sm")
    stz = [p.sem(f"stz{i}") for i in range(4)]
    stu = [p.sem(f"stu{i}") for i in range(2)]
    sdt = [p.sem(f"sdt{i}") for i in range(2)]
    pev = p.sem("pev"); cpe = p.ctr("pev")

    blk = p.begin()

    @blk.vector
    def _(vector):
        vector.memset(ones_k[:], 1.0)
        vector.memset(ones_1[:], 1.0).then_inc(sv, 1)
    ones_done = cv.inc()

    @blk.sync
    def _(sync):
        sync.dma_start(out=cwsb[:], in_=cw).then_inc(ld, 16)
        sync.dma_start(out=cbsb[:], in_=cb).then_inc(ld, 16)
        sync.dma_start(out=dtbsb[:], in_=dtb).then_inc(ld, 16)
    cl.inc(48)

    prev_ln = ones_done
    ln_chunks = [(0, 512, 0)] + [(3 + t * 512, 512, 3 + t * 512) for t in range(4)]
    for (src0, w, dst0) in ln_chunks:
        @blk.sync
        def _(sync, src0=src0, w=w, prev_ln=prev_ln):
            sync.wait_ge(sv, prev_ln)
            sync.dma_start(out=xch[:, :, 0:w],
                           in_=xh[:, src0:src0 + w].rearrange(
                               "(k p) w -> p k w", p=128)).then_inc(ld, 16)
        xld = cl.inc(16)
        prev_ln = emit_ln(
            p, lambda k, w=w: xch[:, k, 0:w], NKD, w,
            lambda k, dst0=dst0, w=w: xnb[:, k, dst0:dst0 + w],
            sqb, (pj[0], pj[1]), bc_ps, stat_sb, ones_k, ones_1, D,
            waits_pre=[(ld, xld), (sv, prev_ln)])
    xnb_ready = prev_ln

    xs_sl = [(0, 512), (512, 1024), (1024, 1536), (1536, 2048), (1539, 2051)]
    z_sl = [(3, 515), (515, 1027), (1027, 1539), (1539, 2051)]
    ev = {}
    convd = {}
    silud = {}
    pevd = {}
    gidx = 0

    for o in range(66):
        kind = "z" if o < 32 else ("xbc" if o < 65 else "dt")
        sl = xs_sl if kind == "xbc" else z_sl

        @blk.gpsimd
        def _(gpsimd, o=o):
            if o >= 2:
                gpsimd.wait_ge(ss, ev[o - 2])
            gpsimd.dma_start(out=wsl[o % 2][:].rearrange("p (k m) -> p k m", m=128),
                             in_=w_ip[o]).then_inc(wl[o % 2], 16)

        for gi in range(2):
            js = (0, 1) if gi == 0 else (2, 3)
            bset = gidx % 3

            @blk.tensor
            def _(tensor, o=o, gi=gi, js=js, bset=bset, gidx=gidx, sl=sl):
                if o == 0 and gi == 0:
                    tensor.wait_ge(sv, xnb_ready)
                if gi == 0:
                    tensor.wait_ge(wl[o % 2], 16 * (o // 2 + 1))
                if gidx >= 3:
                    psem, pval = pevd[gidx - 3]
                    tensor.wait_ge(p.sems[psem], pval)
                ins = None
                for k in range(NKD):
                    for jj, j in enumerate(js):
                        c0, c1 = sl[j]
                        ins = tensor.matmul(pj[2 * bset + jj][:, 0:c1 - c0],
                                            wsl[o % 2][:, k * 128:(k + 1) * 128],
                                            xnb[:, k, c0:c1],
                                            start=(k == 0), stop=(k == NKD - 1))
                ins.then_inc(sm, 1)
            mmd = cm.inc()

            @blk.scalar
            def _(scalar, o=o, gi=gi, js=js, bset=bset, mmd=mmd, sl=sl, kind=kind):
                scalar.wait_ge(sm, mmd)
                ins = None
                if kind == "xbc":
                    if o >= 34 and gi == 0:
                        scalar.wait_ge(ss, silud[(o - 2, 3)])
                    for jj, j in enumerate(js):
                        c0, c1 = sl[j]
                        ins = scalar.copy(xst[o % 2][:, c0:c1],
                                          pj[2 * bset + jj][:, 0:c1 - c0])
                else:
                    oz = o if o < 32 else 65
                    for jj, j in enumerate(js):
                        c0, c1 = sl[j]
                        if o >= 1:
                            scalar.wait_ge(stz[j], 16 * (o if o < 32 else 32))
                        ins = scalar.copy(zst[j][:, :], pj[2 * bset + jj][:, 0:c1 - c0])
                if kind == "z" and gi == 1:
                    ins.then_inc(ss, 1)
                    pevd[gidx] = ("ss", cs.inc())
                    ev[o] = pevd[gidx][1]
                elif kind == "dt" and gi == 1:
                    ins.then_inc(ss, 1)
                    pevd[gidx] = ("ss", cs.inc())
                    ev[o] = pevd[gidx][1]
                else:
                    ins.then_inc(pev, 1)
                    pevd[gidx] = ("pev", cpe.inc())
            gidx += 1

        if kind == "xbc":
            ot = o - 32

            @blk.tensor
            def _(tensor, o=o):
                if o >= 2:
                    tensor.wait_ge(ss, ev[o - 2])
                hb = (o % 2) * 512
                ins = None
                for k in range(NKD):
                    ins = tensor.matmul(bc_ps[:, hb:hb + 512],
                                        wsl[o % 2][:, k * 128:(k + 1) * 128],
                                        xnb[:, k, 1539:2051],
                                        start=(k == 0), stop=(k == NKD - 1))
                ins.then_inc(sm, 1)
            hmmd = cm.inc()

            @blk.scalar
            def _(scalar, o=o, hmmd=hmmd):
                scalar.wait_ge(sm, hmmd)
                psem, pval = pevd[2 * o + 1]
                scalar.wait_ge(p.sems[psem], pval)
                hb = (o % 2) * 512
                scalar.copy(xst[o % 2][:, 1539:2051],
                            bc_ps[:, hb:hb + 512]).then_inc(ss, 1)
            ev[o] = cs.inc()

            for tc in range(4):
                @blk.vector
                def _(vector, o=o, tc=tc, ot=ot):
                    if tc == 0:
                        vector.wait_ge(ss, ev[o])
                        if o == 32:
                            vector.wait_ge(ld, 48)
                    if (o, tc) not in [(32, 0), (32, 1)]:
                        prev = (o, tc - 2) if tc >= 2 else (o - 1, tc + 2)
                        if prev in silud:
                            vector.wait_ge(ss, silud[prev])
                    s = xst[o % 2]
                    c0 = tc * 512
                    vector.tensor_scalar(cacc[0][:], s[:, c0:c0 + 512],
                                         cwsb[:, ot, 0:1], None, ALU.mult)
                    vector.scalar_tensor_tensor(cacc[1][:], s[:, c0 + 1:c0 + 513],
                                                cwsb[:, ot, 1:2], cacc[0][:],
                                                ALU.mult, ALU.add)
                    vector.scalar_tensor_tensor(cacc[2][:], s[:, c0 + 2:c0 + 514],
                                                cwsb[:, ot, 2:3], cacc[1][:],
                                                ALU.mult, ALU.add)
                    vector.scalar_tensor_tensor(cfin[tc % 2][:], s[:, c0 + 3:c0 + 515],
                                                cwsb[:, ot, 3:4], cacc[2][:],
                                                ALU.mult, ALU.add).then_inc(sv, 1)
                convd[(o, tc)] = cv.inc()

                @blk.scalar
                def _(scalar, o=o, tc=tc, ot=ot):
                    scalar.wait_ge(sv, convd[(o, tc)])
                    if o >= 34 and tc == 0:
                        scalar.wait_ge(stu[o % 2], 16 * ((o - 32) // 2))
                    scalar.activation(ub[o % 2][:, tc * 512:(tc + 1) * 512],
                                      cfin[tc % 2][:], AF.Silu,
                                      bias=cbsb[:, ot:ot + 1]).then_inc(ss, 1)
                silud[(o, tc)] = cs.inc()

            @blk.gpsimd
            def _(gpsimd, o=o, ot=ot):
                gpsimd.wait_ge(ss, silud[(o, 3)])
                gpsimd.dma_start(out=xsc_o[ot * 128:(ot + 1) * 128, :],
                                 in_=ub[o % 2][:]).then_inc(stu[o % 2], 16)
        elif kind == "z":
            @blk.sync
            def _(sync, o=o):
                sync.wait_ge(ss, ev[o])
                for j in range(4):
                    sync.dma_start(out=z1_o[o * 128:(o + 1) * 128,
                                           j * 512:(j + 1) * 512],
                                   in_=zst[j][:]).then_inc(stz[j], 16)
        else:
            dt_sv = {}
            for j in range(4):
                @blk.scalar
                def _(scalar, o=o, j=j):
                    if j == 0:
                        scalar.wait_ge(ss, ev[o])
                        scalar.wait_ge(ld, 48)
                    else:
                        scalar.wait_ge(sv, dt_sv[j - 1])
                    scalar.activation(sp[0][0:32, :], zst[j][0:32, :], AF.Relu,
                                      bias=dtbsb[0:32, 0:1])
                    scalar.activation(sp[1][0:32, :], zst[j][0:32, :], AF.Abs,
                                      bias=dtbsb[0:32, 0:1])
                    scalar.activation(sp[2][0:32, :], sp[1][0:32, :], AF.Exp,
                                      scale=-1.0)
                    scalar.activation(sp[3][0:32, :], sp[2][0:32, :], AF.Ln,
                                      bias=1.0).then_inc(ss, 1)
                sp_done = cs.inc()

                @blk.vector
                def _(vector, j=j, sp_done=sp_done):
                    vector.wait_ge(ss, sp_done)
                    if j >= 2:
                        vector.wait_ge(sdt[j % 2], 16 * (j // 2))
                    vector.tensor_add(dtst[j % 2][0:32, :], sp[0][0:32, :],
                                      sp[3][0:32, :]).then_inc(sv, 1)
                dt_sv[j] = cv.inc()

                @blk.sync
                def _(sync, j=j):
                    sync.wait_ge(sv, dt_sv[j])
                    sync.dma_start(out=dt1_o[:, j * 512:(j + 1) * 512],
                                   in_=dtst[j % 2][0:32, :]).then_inc(sdt[j % 2], 16)

    @blk.sync
    def _(sync):
        sync.wait_ge(stu[0], 16 * 17)
        sync.wait_ge(stu[1], 16 * 16)
        sync.wait_ge(stz[0], 16 * 32)
        sync.wait_ge(stz[1], 16 * 32)
        sync.wait_ge(stz[2], 16 * 32)
        sync.wait_ge(stz[3], 16 * 32)
        sync.wait_ge(sdt[0], 16 * 2)
        sync.wait_ge(sdt[1], 16 * 2)
    return p



def build_l1ssd():
    p = Prog("l1ssd")
    nc = p.nc
    NH, NC, CH = 16, 16, 256
    xs_i = nc.dram_tensor("xs_i", [L, 2048], F32, kind="ExternalInput").ap()
    bct_i = nc.dram_tensor("bct_i", [L, 128], F32, kind="ExternalInput").ap()
    bcf_i = nc.dram_tensor("bcf_i", [128, L], F32, kind="ExternalInput").ap()
    dt_i = nc.dram_tensor("dt_i", [16, L], F32, kind="ExternalInput").ap()
    alog_i = nc.dram_tensor("alog_i", [16, 1], F32, kind="ExternalInput").ap()
    dbc_i = nc.dram_tensor("dbc_i", [128, 16], F32, kind="ExternalInput").ap()
    mask_i = nc.dram_tensor("mask_i", [128, 2, 256], F32, kind="ExternalInput").ap()
    eh128_i = nc.dram_tensor("eh128_i", [16, 16 * 128], F32, kind="ExternalInput").ap()
    eh64_i = nc.dram_tensor("eh64_i", [16, 16 * 64], F32, kind="ExternalInput").ap()
    ident_i = nc.dram_tensor("ident_i", [128, 128], F32, kind="ExternalInput").ap()
    y_o = nc.dram_tensor("y_o", [L, 2048], F32, kind="ExternalOutput").ap()
    import os
    dbg = bool(os.environ.get("DBG_SSD"))
    if dbg:
        hdbg = nc.dram_tensor("hdbg", [16, 64, 16, 128], F32, kind="ExternalOutput").ap()
        cdbg = nc.dram_tensor("cdbg", [16, 64, 16], F32, kind="ExternalOutput").ap()
        trdbg = nc.dram_tensor("trdbg", [16, 128, 6, 16], F32, kind="ExternalOutput").ap()
        bctdbg = nc.dram_tensor("bctdbg", [16, 128, 2, 128], BF16, kind="ExternalOutput").ap()
        bddbg = nc.dram_tensor("bddbg", [16, 128, 2, 64], BF16, kind="ExternalOutput").ap()
        hphdbg = nc.dram_tensor("hphdbg", [16, 64, 128], F32, kind="ExternalOutput").ap()
        spdbg = nc.dram_tensor("spdbg", [16, 64, 128], F32, kind="ExternalOutput").ap()
        dsdbg = nc.dram_tensor("dsdbg", [16, 32, 256], F32, kind="ExternalOutput").ap()
        dstdbg = nc.dram_tensor("dstdbg", [16, 32, 256], F32, kind="ExternalOutput").ap()

    xs_ch = p.sbuf("xs_ch", [128, 2, 2048], BF16)
    bct_ch = p.sbuf("bct_ch", [128, 2, 128], BF16)
    bcf_B = p.sbuf("bcf_B", [64, 256], BF16)
    bcf_C = p.sbuf("bcf_C", [64, 256], BF16)
    dt_sb = p.sbuf("dt_sb", [16, L], F32)
    a1_sb = p.sbuf("a1_sb", [16, 1], F32)
    ea_sb = p.sbuf("ea_sb", [16, 1], F32)
    ones16 = p.sbuf("ones16", [16, 256], F32)
    ones1 = p.sbuf("ones1", [1, 64], F32)
    dbc_sb = p.sbuf("dbc_sb", [128, 16], F32)
    mask_sb = p.sbuf("mask_sb", [128, 2, 256], F32)
    eh128 = p.sbuf("eh128", [16, 16 * 128], F32)
    eh64 = p.sbuf("eh64", [16, 16 * 64], F32)
    idsb = p.sbuf("idsb", [128, 128], F32)
    dtA = p.sbuf("dtA", [16, 256], F32)
    cumA = p.sbuf("cumA", [16, 256], F32)
    dsrow32 = p.sbuf("dsrow32", [32, 256], F32)
    dstmp32 = p.sbuf("dstmp32", [32, 256], F32)
    sdo32 = p.sbuf("sdo32", [32, 256], F32)
    sdotmp32 = p.sbuf("sdotmp32", [32, 256], F32)
    dsrow = dsrow32[0:16, :]
    dstmp = dstmp32[0:16, :]
    sdo_row = sdo32[0:16, :]
    trev = p.sbuf("trev", [128, 6, 16], F32)
    trev_bf = p.sbuf("trev_bf", [128, 6, 16], BF16)
    cdrow_sb = p.sbuf("cdrow_sb", [1, 16], F32)
    cdbc_sb = p.sbuf("cdbc_sb", [64, 16], F32)
    bctm = p.sbuf("bctm", [128, 2, 256], BF16)
    e1 = p.sbuf("e1", [128, 2, 256], F32)
    e2 = p.sbuf("e2", [128, 2, 256], BF16)
    mt = p.sbuf("mt", [128, 2, 256], BF16)
    xsd = p.sbuf("xsd", [128, 2, 128], BF16)
    bd = p.sbuf("bd", [128, 2, 64], BF16)
    sdo_sb = p.sbuf("sdo_sb", [64, 256], BF16)
    cdt = p.sbuf("cdt", [64, 256], F32)
    hb = [p.sbuf(f"hb{i}", [64, 16, 128], F32) for i in range(2)]
    t1f = p.sbuf("t1f", [128, 2, 128], F32)
    spdbg_sb = p.sbuf("spdbg_sb", [64, 16, 128], F32)
    yout = [p.sbuf(f"yout{i}", [128, 2, 128], F32) for i in range(2)]

    yps_f = p.psum("yps", [128, 512], F32)
    sps_f = p.psum("sps", [128, 512], F32)
    rowps_f = p.psum("rowps", [128, 512], F32)
    sdops_f = p.psum("sdops", [128, 512], F32)
    bctps_f = p.psum("bctps", [128, 512], F32)
    trps_f = p.psum("trps", [128, 512], F32)
    cdps_f = p.psum("cdps", [128, 512], F32)
    yps = yps_f[:, 0:256]
    sps = sps_f[0:64, 0:128]
    rowps = rowps_f[:, 0:256]
    sdops = sdops_f[0:64, 0:256]
    bctps = bctps_f[:, 0:512].rearrange("p (sc w) -> p sc w", sc=2)
    trps = trps_f[:, 0:128]
    cdps = cdps_f[0:64, 0:16]

    ld = p.sem("ld"); cl = p.ctr("ld")
    gb = p.sem("gb"); cgb = p.ctr("gb")
    sv, ss, sm = p.sem("sv"), p.sem("ss"), p.sem("sm")
    cv, cs, cm = p.ctr("sv"), p.ctr("ss"), p.ctr("sm")
    sty = [p.sem(f"sty{i}") for i in range(2)]

    blk = p.begin()

    @blk.sync
    def _(sync):
        sync.dma_start(out=dt_sb[:], in_=dt_i).then_inc(ld, 16)
        sync.dma_start(out=a1_sb[:], in_=alog_i).then_inc(ld, 16)
        sync.dma_start(out=eh128[:], in_=eh128_i).then_inc(ld, 16)
        sync.dma_start(out=eh64[:], in_=eh64_i).then_inc(ld, 16)
        sync.dma_start(out=idsb[:], in_=ident_i).then_inc(ld, 16)
        sync.dma_start(out=mask_sb[:], in_=mask_i).then_inc(ld, 16)
        sync.dma_start(out=dbc_sb[:], in_=dbc_i).then_inc(ld, 16)
    consts = cl.inc(112)

    @blk.vector
    def _(vector):
        vector.memset(hb[0][:], 0.0)
        vector.memset(ones16[:], 1.0)
        vector.memset(dstmp32[:], 0.0)
        vector.memset(sdotmp32[:], 0.0)
        vector.memset(ones1[:], 1.0).then_inc(sv, 1)
    h0done = cv.inc()

    @blk.scalar
    def _(scalar):
        scalar.wait_ge(ld, consts)
        scalar.activation(ea_sb[:], a1_sb[:], AF.Exp).then_inc(ss, 1)
    ea_done = cs.inc()

    @blk.vector
    def _(vector):
        vector.wait_ge(ss, ea_done)
        vector.tensor_scalar(a1_sb[:], ea_sb[:], -1.0, None, ALU.mult).then_inc(sv, 1)
    a1_done = cv.inc()

    yfin = {}
    ydone = {}
    sdone = {}
    hupd = {}
    e1d = {}
    bctm_d = {}
    for c in range(NC):
        c0 = c * CH
        cur, nxt = c % 2, (c + 1) % 2
        wait_loads = ([(sv, a1_done), (sv, h0done)] if c == 0
                      else [(sv, yfin[(c - 1, NH - 1)])])

        @blk.gpsimd
        def _(g, c0=c0, wait_loads=wait_loads):
            _waits(g, wait_loads)
            g.dma_start(out=xs_ch[:], in_=xs_i[c0:c0 + CH, :].rearrange(
                "(sc p) d -> p sc d", p=128)).then_inc(gb, 16)
            g.dma_start(out=bct_ch[:], in_=bct_i[c0:c0 + CH, :].rearrange(
                "(sc p) d -> p sc d", p=128)).then_inc(gb, 16)
            g.dma_start(out=bcf_B[:], in_=bcf_i[0:64, c0:c0 + CH]).then_inc(gb, 16)
            g.dma_start(out=bcf_C[:], in_=bcf_i[64:128, c0:c0 + CH]).then_inc(gb, 16)
        gls = cgb.inc(64)

        @blk.vector
        def _(vector, c=c, c0=c0, wait_loads=wait_loads):
            _waits(vector, wait_loads)
            vector.tensor_scalar(dtA[:], dt_sb[:, c0:c0 + CH], a1_sb[0:16, 0:1],
                                 None, ALU.mult)
            vector.tensor_tensor_scan(cumA[:], ones16[:], dtA[:], 0.0,
                                      ALU.mult, ALU.add).then_inc(sv, 1)
        scan_d = cv.inc()

        @blk.vector
        def _(vector, scan_d=scan_d):
            vector.wait_ge(sv, scan_d)
            vector.tensor_copy(sdotmp32[0:16, :], cumA[:])
            vector.tensor_scalar(dstmp[:], cumA[:], -1.0, cumA[0:16, 255:256],
                                 ALU.mult, ALU.add).then_inc(sv, 1)
        dst_d = cv.inc()

        @blk.scalar
        def _(scalar, dst_d=dst_d):
            scalar.wait_ge(sv, dst_d)
            scalar.activation(dsrow32[:], dstmp32[:], AF.Exp)
            scalar.activation(sdo32[:], sdotmp32[:], AF.Exp).then_inc(ss, 1)
        exps_d = cs.inc()

        if dbg:
            @blk.sync
            def _(sync, c=c, exps_d=exps_d):
                sync.wait_ge(p.sems["ss"], exps_d)
                sync.dma_start(out=dsdbg[c], in_=dsrow32[:]).then_inc(p.sem("dbgs"), 16)
                sync.dma_start(out=dstdbg[c], in_=dstmp32[:]).then_inc(p.sem("dbgs"), 16)

        tr_srcs = [cumA, cumA, dsrow, dsrow, None, None]
        tr_ev = None
        for ti in range(6):
            sc = ti % 2

            @blk.tensor
            def _(tensor, c=c, c0=c0, ti=ti, sc=sc, exps_d=exps_d, tr_ev=tr_ev,
                  wait_loads=wait_loads):
                if ti == 0:
                    tensor.wait_ge(ss, exps_d)
                    _waits(tensor, wait_loads)
                else:
                    tensor.wait_ge(ss, tr_ev)
                if ti < 4:
                    srcv = tr_srcs[ti][:, sc * 128:(sc + 1) * 128]
                else:
                    srcv = dt_sb[:, c0 + sc * 128:c0 + (sc + 1) * 128]
                tensor.transpose(trps_f[:, 0:16], srcv, idsb[0:16, 0:16]).then_inc(sm, 1)
            trd = cm.inc()

            @blk.scalar
            def _(scalar, ti=ti, trd=trd):
                scalar.wait_ge(sm, trd)
                scalar.copy(trev[:, ti, :], trps_f[:, 0:16]).then_inc(ss, 1)
            tr_ev = cs.inc()
        @blk.scalar
        def _(scalar):
            scalar.copy(trev_bf[:], trev[:]).then_inc(ss, 1)
        tr_ev = cs.inc()
        tr_all = tr_ev

        if dbg:
            @blk.sync
            def _(sync, c=c, tr_all=tr_all):
                sync.wait_ge(p.sems["ss"], tr_all)
                sync.dma_start(out=trdbg[c], in_=trev[:]).then_inc(p.sem("dbgs"), 16)

        @blk.tensor
        def _(tensor, tr_all=tr_all):
            tensor.wait_ge(ss, tr_all)
            tensor.matmul(trps_f[0:1, 0:16], cumA[0:16, 255:256], idsb[0:16, 0:16],
                          start=True, stop=True).then_inc(sm, 1)
        cdr_d = cm.inc()

        @blk.scalar
        def _(scalar, cdr_d=cdr_d):
            scalar.wait_ge(sm, cdr_d)
            scalar.copy(cdrow_sb[:], trps_f[0:1, 0:16]).then_inc(ss, 1)
        cdr_sb = cs.inc()

        @blk.tensor
        def _(tensor, cdr_sb=cdr_sb, c=c):
            tensor.wait_ge(ss, cdr_sb)
            if c >= 1:
                tensor.wait_ge(sv, hupd[(c - 1, NH - 1)])
            tensor.matmul(cdps_f[0:64, 0:16], ones1[0:1, :], cdrow_sb[:],
                          start=True, stop=True).then_inc(sm, 1)
        cdb_d = cm.inc()

        @blk.scalar
        def _(scalar, cdb_d=cdb_d):
            scalar.wait_ge(sm, cdb_d)
            scalar.activation(cdbc_sb[:], cdps_f[0:64, 0:16], AF.Exp).then_inc(ss, 1)
        cdb_sb = cs.inc()

        @blk.tensor
        def _(tensor, c=c, gls=gls):
            tensor.wait_ge(gb, gls)
            if c >= 1:
                tensor.wait_ge(sv, bctm_d[c - 1])
            ins = None
            for sc in range(2):
                ins = tensor.matmul(bctps[:, sc, :],
                                    bcf_B[:, sc * 128:(sc + 1) * 128],
                                    bcf_C[:],
                                    start=True, stop=True)
            ins.then_inc(sm, 1)
        bct_d = cm.inc()

        @blk.vector
        def _(vector, c=c, bct_d=bct_d):
            vector.wait_ge(sm, bct_d)
            ins = None
            for sc in range(2):
                ins = vector.tensor_tensor(bctm[:, sc, :], bctps[:, sc, :],
                                           mask_sb[:, sc, :], ALU.mult)
            ins.then_inc(sv, 1)
        bctm_d[c] = cv.inc()

        for h in range(NH):
            gh = c * NH + h
            par = gh % 2
            hc = slice(h * 128, (h + 1) * 128)

            @blk.tensor
            def _(tensor, c=c, h=h, gh=gh):
                if gh >= 1:
                    pk = (c, h - 1) if h >= 1 else (c - 1, NH - 1)
                    tensor.wait_ge(sv, e1d[pk])
                if h == 0:
                    tensor.wait_ge(sv, bctm_d[c])
                tensor.matmul(rowps_f[:, 0:256], eh128[:, h * 128:(h + 1) * 128], cumA[:],
                              start=True, stop=True).then_inc(sm, 1)
            row_d = cm.inc()

            @blk.vector
            def _(vector, c=c, h=h, gh=gh, row_d=row_d):
                vector.wait_ge(sm, row_d)
                if gh >= 1:
                    pk = (c, h - 1) if h >= 1 else (c - 1, NH - 1)
                    vector.wait_ge(sm, ydone[pk])
                ins = None
                for sc in range(2):
                    ins = vector.tensor_scalar(e1[:, sc, :], rowps_f[:, 0:256],
                                               trev[:, sc, h:h + 1], 0.0,
                                               ALU.subtract, ALU.min)
                ins.then_inc(sv, 1)
            e1d[(c, h)] = cv.inc()

            @blk.scalar
            def _(scalar, h=h, e1dv=e1d[(c, h)]):
                scalar.wait_ge(sv, e1dv)
                ins = None
                for sc in range(2):
                    ins = scalar.activation(e2[:, sc, :], e1[:, sc, :], AF.Exp)
                ins.then_inc(ss, 1)
            e2d = cs.inc()

            @blk.vector
            def _(vector, c=c, h=h, gh=gh, e2d=e2d, hc=hc):
                vector.wait_ge(ss, e2d)
                if gh >= 1:
                    pk = (c, h - 1) if h >= 1 else (c - 1, NH - 1)
                    vector.wait_ge(sm, sdone[pk])
                for sc in range(2):
                    vector.tensor_tensor(mt[:, sc, :], e2[:, sc, :],
                                         bctm[:, sc, :], ALU.mult)
                ins = None
                for sc in range(2):
                    ins = vector.tensor_scalar(xsd[:, sc, :], xs_ch[:, sc, hc],
                                               trev[:, 4 + sc, h:h + 1], None,
                                               ALU.mult)
                for sc in range(2):
                    ins = vector.tensor_scalar(bd[:, sc, :], bct_ch[:, sc, 0:64],
                                               trev[:, 2 + sc, h:h + 1], None,
                                               ALU.mult)
                ins.then_inc(sv, 1)
            prep_d = cv.inc()

            if dbg and h == 0:
                @blk.sync
                def _(sync, c=c, prep_d=prep_d):
                    sync.wait_ge(sv, prep_d)
                    sync.dma_start(out=bddbg[c], in_=bd[:]).then_inc(p.sem("dbgs"), 16)

            @blk.tensor
            def _(tensor, c=c, h=h, gh=gh, prep_d=prep_d):
                tensor.wait_ge(sv, prep_d)
                if gh >= 1:
                    pk = (c, h - 1) if h >= 1 else (c - 1, NH - 1)
                    tensor.wait_ge(sv, yfin[pk])
                first = True
                for lc in range(2):
                    for sc in range(2):
                        tensor.matmul(yps[:, lc * 128:(lc + 1) * 128],
                                      mt[:, sc, lc * 128:(lc + 1) * 128],
                                      xsd[:, sc, :],
                                      start=first, stop=False)
                        first = False
                if gh >= 1:
                    pk = (c, h - 1) if h >= 1 else (c - 1, NH - 1)
                    tensor.wait_ge(sv, hupd[pk])
                ins = None
                for sc in range(2):
                    ins = tensor.matmul(sps_f[0:64, 0:128], bd[:, sc, :], xsd[:, sc, :],
                                        start=(sc == 0), stop=(sc == 1))
                ins.then_inc(sm, 1)
            sdone[(c, h)] = cm.inc()

            @blk.tensor
            def _(tensor, c=c, h=h, gh=gh):
                if gh >= 1:
                    pk = (c, h - 1) if h >= 1 else (c - 1, NH - 1)
                    tensor.wait_ge(ss, sdoev[pk])
                tensor.matmul(sdops_f[0:64, 0:256], eh64[:, h * 64:(h + 1) * 64], sdo_row[:],
                              start=True, stop=True).then_inc(sm, 1)
            sdo_d = cm.inc()

            @blk.scalar
            def _(scalar, sdo_d=sdo_d):
                scalar.wait_ge(sm, sdo_d)
                scalar.copy(sdo_sb[:], sdops_f[0:64, 0:256]).then_inc(ss, 1)
            sdoev = sdoev if gh > 0 else {}
            sdoev[(c, h)] = cs.inc()

            @blk.vector
            def _(vector, c=c, h=h, gh=gh, sdv=sdoev[(c, h)]):
                vector.wait_ge(ss, sdv)
                if gh >= 1:
                    pk = (c, h - 1) if h >= 1 else (c - 1, NH - 1)
                    vector.wait_ge(sm, ydone[pk])
                vector.tensor_tensor(cdt[:], bcf_C[:], sdo_sb[:],
                                     ALU.mult).then_inc(sv, 1)
            cdt_d = cv.inc()

            @blk.tensor
            def _(tensor, c=c, h=h, cdt_d=cdt_d, cur=cur):
                tensor.wait_ge(sv, cdt_d)
                ins = None
                for lc in range(2):
                    ins = tensor.matmul(yps[:, lc * 128:(lc + 1) * 128],
                                        cdt[0:64, lc * 128:(lc + 1) * 128],
                                        hb[cur][:, h, :],
                                        start=False, stop=(lc == 1))
                ins.then_inc(sm, 1)
            ydone[(c, h)] = cm.inc()

            @blk.vector
            def _(vector, c=c, h=h, cur=cur, nxt=nxt, sdn=sdone[(c, h)],
                  cdb_sb=cdb_sb):
                vector.wait_ge(sm, sdn)
                if c == 0 and h == 0:
                    vector.wait_ge(ss, cdb_sb)
                vector.scalar_tensor_tensor(hb[nxt][:, h, :], hb[cur][:, h, :],
                                            cdbc_sb[:, h:h + 1], sps_f[0:64, 0:128],
                                            ALU.mult, ALU.add).then_inc(sv, 1)
            hupd[(c, h)] = cv.inc()

            if dbg and c == 0:
                @blk.scalar
                def _(scalar, h=h, sdn=sdone[(c, h)]):
                    scalar.wait_ge(sm, sdn)
                    scalar.copy(spdbg_sb[:, h, :], sps_f[0:64, 0:128]).then_inc(
                        p.sem("dbg2"), 1)

                @blk.sync
                def _(sync, h=h, hupv=hupd[(c, h)]):
                    sync.wait_ge(sv, hupv)
                    sync.dma_start(out=hphdbg[h], in_=hb[1][:, h, :]).then_inc(
                        p.sem("dbgs"), 16)
                    sync.wait_ge(p.sem("dbg2"), h + 1)
                    sync.dma_start(out=spdbg[h], in_=spdbg_sb[:, h, :]).then_inc(
                        p.sem("dbgs"), 16)

            @blk.vector
            def _(vector, c=c, h=h, gh=gh, par=par, hc=hc, ydn=ydone[(c, h)]):
                vector.wait_ge(sm, ydn)
                if gh >= 2:
                    vector.wait_ge(sty[par], 16 * (gh // 2))
                ins = None
                for lc in range(2):
                    vector.tensor_scalar(t1f[:, lc, :], xs_ch[:, lc, hc],
                                         dbc_sb[:, h:h + 1], None, ALU.mult)
                    ins = vector.tensor_add(yout[par][:, lc, :], t1f[:, lc, :],
                                            yps[:, lc * 128:(lc + 1) * 128])
                ins.then_inc(sv, 1)
            yfin[(c, h)] = cv.inc()

            @blk.sync
            def _(sync, c0=c0, h=h, par=par, yfv=yfin[(c, h)]):
                sync.wait_ge(sv, yfv)
                sync.dma_start(
                    out=y_o[c0:c0 + CH, h * 128:(h + 1) * 128].rearrange(
                        "(lc p) d -> p lc d", p=128),
                    in_=yout[par][:]).then_inc(sty[par], 16)

        if dbg:
            @blk.sync
            def _(sync, c=c, nxt=nxt, hupv=hupd[(c, NH - 1)], cdbv=cdb_sb):
                sync.wait_ge(sv, hupv)
                sync.dma_start(out=hdbg[c], in_=hb[nxt][:]).then_inc(p.sem("dbgs"), 16)
                sync.dma_start(out=cdbg[c], in_=cdbc_sb[:]).then_inc(p.sem("dbgs"), 16)
                sync.dma_start(out=bctdbg[c], in_=bct_ch[:]).then_inc(p.sem("dbgs"), 16)

    @blk.sync
    def _(sync):
        sync.wait_ge(sty[0], 16 * 128)
        sync.wait_ge(sty[1], 16 * 128)
        if dbg:
            pass
    return p



def build_l1out():
    p = Prog("l1out")
    nc = p.nc
    y_i = nc.dram_tensor("y_i", [4096, TOK], F32, kind="ExternalInput").ap()
    z_i = nc.dram_tensor("z_i", [4096, TOK], F32, kind="ExternalInput").ap()
    nw_i = nc.dram_tensor("nw_i", [128, 32], F32, kind="ExternalInput").ap()
    w_op = nc.dram_tensor("w_op", [16, 128, 32, 128], F32, kind="ExternalInput").ap()
    x2_i = nc.dram_tensor("x2_i", [D, TOK], F32, kind="ExternalInput").ap()
    x3_o = nc.dram_tensor("x3_o", [D, TOK], F32, kind="ExternalOutput").ap()

    ych = p.sbuf("ych", [128, 32, 512], BF16)
    zch = p.sbuf("zch", [128, 32, 512], BF16)
    gch = p.sbuf("gch", [128, 32, 512], BF16)
    gnb = p.sbuf("gnb", [128, 32, 512], BF16)
    szb = p.sbuf("szb", [128, 512], BF16)
    sqb = p.sbuf("sqg", [128, 512], F32)
    nwsb = p.sbuf("nwsb", [128, 32], F32)
    rbb = p.sbuf("rbb", [128, 512], BF16)
    stat_sb = p.sbuf("stat", [1, 6 * 512], F32)
    ones_k = p.sbuf("ones_k", [128, 1], F32)
    ones_1 = p.sbuf("ones_1", [1, 128], F32)
    wsl = [p.sbuf(f"wsl{i}", [128, 32 * 128], BF16) for i in range(2)]
    xres = [p.sbuf(f"xres{i}", [128, 512], F32) for i in range(2)]
    outs = [p.sbuf(f"outs{i}", [128, 512], F32) for i in range(2)]
    banks = [p.psum(f"pb{i}", [128, 512], F32) for i in range(6)]
    bc_ps = p.psum("bc_ps", [128, 1024], F32)

    ld = p.sem("ld"); cl = p.ctr("ld")
    gl = p.sem("gl"); cgl = p.ctr("gl")
    sv, ss, sm = p.sem("sv"), p.sem("ss"), p.sem("sm")
    cv, cs, cm = p.ctr("sv"), p.ctr("ss"), p.ctr("sm")
    lx = [p.sem(f"lx{i}") for i in range(2)]
    sto = [p.sem(f"sto{i}") for i in range(2)]
    sev = p.sem("g_ev"); cev = p.ctr("g_ev")

    blk = p.begin()

    @blk.vector
    def _(vector):
        vector.memset(ones_k[:], 1.0)
        vector.memset(ones_1[:], 1.0).then_inc(sv, 1)
    ones_done = cv.inc()

    @blk.sync
    def _(sync):
        sync.dma_start(out=nwsb[:], in_=nw_i).then_inc(ld, 16)
    cl.inc(16)

    evds = {}
    gn_done = {}
    for tc in range(4):
        c0 = tc * 512
        wait_prev = [(sv, ones_done)] if tc == 0 else [(sm, p.ctrs["sm"].v)]

        @blk.gpsimd
        def _(g, c0=c0, tc=tc, wait_prev=wait_prev):
            _waits(g, wait_prev)
            g.dma_start(out=ych[:], in_=y_i[:, c0:c0 + 512].rearrange(
                "(k p) w -> p k w", p=128)).then_inc(gl, 16)
            g.dma_start(out=zch[:], in_=z_i[:, c0:c0 + 512].rearrange(
                "(k p) w -> p k w", p=128)).then_inc(gl, 16)
        gls = cgl.inc(32)

        sil_d = {}
        for k in range(32):
            @blk.scalar
            def _(scalar, k=k, tc=tc, gls=gls):
                if k == 0:
                    scalar.wait_ge(gl, gls)
                    scalar.wait_ge(ld, 16)
                    if tc > 0:
                        scalar.wait_ge(sv, gn_done[tc - 1])
                scalar.activation(szb[:], zch[:, k, :], AF.Silu).then_inc(ss, 1)
            szd = cs.inc()

            @blk.vector
            def _(vector, k=k, szd=szd):
                vector.wait_ge(ss, szd)
                vector.tensor_tensor(gch[:, k, :], ych[:, k, :], szb[:],
                                     ALU.mult).then_inc(sv, 1)
            gd = cv.inc()

            @blk.scalar
            def _(scalar, k=k, gd=gd, tc=tc):
                scalar.wait_ge(sv, gd)
                if k >= 1:
                    scalar.wait_ge(sm, smm_base + k)
                scalar.activation(sqb[:], gch[:, k, :], AF.Square).then_inc(ss, 1)
            sil_d[k] = cs.inc()
            if k == 0:
                smm_base = p.ctrs["sm"].v

            @blk.tensor
            def _(tensor, k=k, tc=tc):
                tensor.wait_ge(ss, sil_d[k])
                tensor.matmul(banks[0][0:1, :], ones_k[:, 0:1], sqb[:],
                              start=(k == 0), stop=(k == 31)).then_inc(sm, 1)
            cm.inc()
        stats_d = p.ctrs["sm"].v

        @blk.vector
        def _(vector, stats_d=stats_d):
            vector.wait_ge(sm, stats_d)
            mu = stat_sb[0:1, 0:512]
            vector.tensor_scalar(mu, banks[0][0:1, :], 1.0 / 4096.0, 1e-5,
                                 ALU.mult, ALU.add).then_inc(sv, 1)
        mu_d = cv.inc()

        @blk.scalar
        def _(scalar, mu_d=mu_d):
            scalar.wait_ge(sv, mu_d)
            scalar.activation(stat_sb[0:1, 512:1024], stat_sb[0:1, 0:512],
                              AF.Sqrt).then_inc(ss, 1)
        sq_d = cs.inc()

        @blk.vector
        def _(vector, sq_d=sq_d):
            vector.wait_ge(ss, sq_d)
            vector.reciprocal(stat_sb[0:1, 1024:1536], stat_sb[0:1, 512:1024]
                              ).then_inc(sv, 1)
        rs_d = cv.inc()

        @blk.tensor
        def _(tensor, rs_d=rs_d):
            tensor.wait_ge(sv, rs_d)
            tensor.matmul(bc_ps[:, 0:512], ones_1[0:1, :], stat_sb[0:1, 1024:1536],
                          start=True, stop=True).then_inc(sm, 1)
        bc_d = cm.inc()

        @blk.scalar
        def _(scalar, bc_d=bc_d, tc=tc):
            scalar.wait_ge(sm, bc_d)
            if tc > 0:
                scalar.wait_ge(sv, gn_done[tc - 1])
            scalar.copy(rbb[:], bc_ps[:, 0:512]).then_inc(ss, 1)
        rb_d = cs.inc()

        @blk.vector
        def _(vector, rb_d=rb_d):
            vector.wait_ge(ss, rb_d)
            ins = None
            for k in range(32):
                ins = vector.scalar_tensor_tensor(gnb[:, k, :], gch[:, k, :],
                                                  nwsb[:, k:k + 1], rbb[:],
                                                  ALU.mult, ALU.mult)
            ins.then_inc(sv, 1)
        gn_done[tc] = cv.inc()

        def evac(o, bs, mmd, tc=tc, c0=c0):
            g2 = tc * 16 + o

            @blk.sync
            def _(sync, o=o, g2=g2, c0=c0):
                if g2 >= 2:
                    sync.wait_ge(sev, evds[g2 - 2])
                sync.dma_start(out=xres[g2 % 2][:],
                               in_=x2_i[o * 128:(o + 1) * 128, c0:c0 + 512]
                               ).then_inc(lx[g2 % 2], 16)

            @blk.vector
            def _(vector, o=o, bs=bs, mmd=mmd, g2=g2):
                vector.wait_ge(p.sems["sm"], mmd)
                vector.wait_ge(lx[g2 % 2], 16 * (g2 // 2 + 1))
                if g2 >= 2:
                    vector.wait_ge(sto[g2 % 2], 16 * (g2 // 2))
                vector.tensor_add(outs[g2 % 2][:], bs[0][:, :],
                                  xres[g2 % 2][:]).then_inc(sev, 1)
            evds[g2] = cev.inc()

            @blk.sync
            def _(sync, o=o, g2=g2, c0=c0):
                sync.wait_ge(sev, evds[g2])
                sync.dma_start(out=x3_o[o * 128:(o + 1) * 128, c0:c0 + 512],
                               in_=outs[g2 % 2][:]).then_inc(sto[g2 % 2], 16)
            return evds[g2]

        emit_mm_stream(p, f"g_{tc}", w_op, 16, 32, [(0, 512)],
                       lambda k, a, b: gnb[:, k, a:b], evac,
                       banks, wsl, pre_wait=[(sv, gn_done[tc])], ev_name="g_ev")

    @blk.sync
    def _(sync):
        sync.wait_ge(sto[0], 16 * 32)
        sync.wait_ge(sto[1], 16 * 32)
    return p


def build_fp4deq():
    p = Prog("fp4deq")
    nc = p.nc
    gpk = nc.dram_tensor("gpk", [1408, 256], I32, kind="ExternalInput").ap()
    gsc = nc.dram_tensor("gsc", [1408, 16], F16, kind="ExternalInput").ap()
    dpk = nc.dram_tensor("dpk", [256, 704], I32, kind="ExternalInput").ap()
    dsc = nc.dram_tensor("dsc", [256, 44], F16, kind="ExternalInput").ap()
    wgu_o = nc.dram_tensor("wgu_o", [1408, 2048], BF16, kind="ExternalOutput").ap()
    wdn_o = nc.dram_tensor("wdn_o", [256, 5632], BF16, kind="ExternalOutput").ap()

    NW = 352
    NB = 8 * NW
    pk = p.sbuf("pk", [128, NW], I32)
    nib = p.sbuf("nib", [128, NB], U8)
    m8 = p.sbuf("m8", [128, NB], U8)
    o8 = p.sbuf("o8", [128, NB], U8)
    s8 = p.sbuf("s8", [128, NB], U8)
    b = [p.sbuf(f"b{i}", [128, NB], F32) for i in range(7)]
    scb = p.sbuf("scb", [128, 22], F32)
    wv = [p.sbuf(f"wv{i}", [128, NB], BF16) for i in range(2)]

    ld = p.sem("ld"); cl = p.ctr("ld")
    sv, ss = p.sem("sv"), p.sem("ss")
    cv, cs = p.ctr("sv"), p.ctr("ss")
    stw = [p.sem(f"stw{i}") for i in range(2)]

    blk = p.begin()
    tiles = ([("g", i, 0, 256) for i in range(11)]
             + [("d", i, hf, 352) for i in range(2) for hf in range(2)])
    prev_done = 0
    prev_p2 = 0
    for ti, (mat, idx, hf, W) in enumerate(tiles):
        N = 8 * W
        G = N // 128

        @blk.sync
        def _(sync, ti=ti, mat=mat, idx=idx, hf=hf, W=W, prev_done=prev_done):
            if ti > 0:
                sync.wait_ge(sv, prev_done)
            if mat == "g":
                sync.dma_start(out=pk[:, 0:W], in_=gpk[idx * 128:(idx + 1) * 128, :]
                               ).then_inc(ld, 16)
            else:
                sync.dma_start(out=pk[:, 0:W],
                               in_=dpk[idx * 128:(idx + 1) * 128,
                                       hf * 352:(hf + 1) * 352]).then_inc(ld, 16)

        @blk.gpsimd
        def _(g, ti=ti, mat=mat, idx=idx, hf=hf, G=G, prev_done=prev_done):
            if ti > 0:
                g.wait_ge(sv, prev_done)
            if mat == "g":
                g.dma_start(out=scb[:, 0:G], in_=gsc[idx * 128:(idx + 1) * 128, :]
                            ).then_inc(p.sem("gld"), 16)
            else:
                g.dma_start(out=scb[:, 0:G],
                            in_=dsc[idx * 128:(idx + 1) * 128,
                                    hf * 22:(hf + 1) * 22]).then_inc(p.sem("gld"), 16)
        lds = cl.inc(16)
        glds = p.ctr("gld").inc(16)

        @blk.vector
        def _(vector, ti=ti, W=W, N=N, lds=lds, glds=glds, pp=prev_p2):
            vector.wait_ge(ld, lds)
            vector.wait_ge(p.sems["gld"], glds)
            if pp:
                vector.wait_ge(ss, pp)
            if ti >= 2:
                vector.wait_ge(stw[ti % 2], 16 * (ti // 2))
            pk8 = pk[:, 0:W].bitcast(U8)
            vector.tensor_scalar(nib[:, 0:N:2], pk8, 15, None, ALU.bitwise_and)
            vector.tensor_scalar(nib[:, 1:N:2], pk8, 4, None,
                                 ALU.logical_shift_right)
            vector.tensor_scalar(m8[:, 0:N], nib[:, 0:N], 7, None, ALU.bitwise_and)
            vector.tensor_scalar(o8[:, 0:N], nib[:, 0:N], 1, None, ALU.bitwise_and)
            vector.tensor_scalar(s8[:, 0:N], nib[:, 0:N], 8, None, ALU.bitwise_and)
            vector.tensor_copy(b[0][:, 0:N], m8[:, 0:N])
            vector.tensor_copy(b[1][:, 0:N], o8[:, 0:N])
            vector.tensor_copy(b[2][:, 0:N], s8[:, 0:N])
            vector.tensor_scalar(b[3][:, 0:N], b[2][:, 0:N], -0.25, 1.0,
                                 ALU.mult, ALU.add)
            vector.tensor_scalar(b[2][:, 0:N], b[0][:, 0:N], 2.0, None,
                                 ALU.is_lt)
            vector.tensor_tensor(b[4][:, 0:N], b[0][:, 0:N], b[2][:, 0:N],
                                 ALU.mult)
            vector.tensor_tensor(b[5][:, 0:N], b[0][:, 0:N], b[1][:, 0:N],
                                 ALU.subtract)
            vector.tensor_scalar(b[6][:, 0:N], b[5][:, 0:N], 0.34657359028,
                                 -0.69314718056, ALU.mult, ALU.add
                                 ).then_inc(sv, 1)
        me2_d = cv.inc()

        @blk.scalar
        def _(scalar, ti=ti, N=N, me2_d=me2_d):
            scalar.wait_ge(sv, me2_d)
            scalar.activation(b[5][:, 0:N], b[6][:, 0:N], AF.Exp).then_inc(ss, 1)
        p2_d = cs.inc()
        prev_p2 = p2_d

        @blk.vector
        def _(vector, ti=ti, N=N, G=G, p2_d=p2_d):
            vector.wait_ge(ss, p2_d)
            vector.tensor_tensor(b[6][:, 0:N], b[5][:, 0:N], b[1][:, 0:N],
                                 ALU.mult)
            vector.scalar_tensor_tensor(b[1][:, 0:N], b[6][:, 0:N], 0.5,
                                        b[5][:, 0:N], ALU.mult, ALU.add)
            vector.scalar_tensor_tensor(b[6][:, 0:N], b[2][:, 0:N], -0.5,
                                        b[1][:, 0:N], ALU.mult, ALU.add)
            vector.scalar_tensor_tensor(b[1][:, 0:N], b[4][:, 0:N], 0.25,
                                        b[6][:, 0:N], ALU.mult, ALU.add)
            vector.tensor_tensor(b[5][:, 0:N], b[1][:, 0:N], b[3][:, 0:N],
                                 ALU.mult)
            ins = None
            for g in range(G):
                ins = vector.tensor_scalar(wv[ti % 2][:, g * 128:(g + 1) * 128],
                                           b[5][:, g * 128:(g + 1) * 128],
                                           scb[:, g:g + 1], None, ALU.mult)
            ins.then_inc(sv, 1)
        prev_done = cv.inc()

        @blk.sync
        def _(sync, ti=ti, mat=mat, idx=idx, hf=hf, N=N, prev_done=prev_done):
            sync.wait_ge(sv, prev_done)
            if mat == "g":
                sync.dma_start(out=wgu_o[idx * 128:(idx + 1) * 128, :],
                               in_=wv[ti % 2][:, 0:N]).then_inc(stw[ti % 2], 16)
            else:
                sync.dma_start(out=wdn_o[idx * 128:(idx + 1) * 128,
                                         hf * 2816:(hf + 1) * 2816],
                               in_=wv[ti % 2][:, 0:N]).then_inc(stw[ti % 2], 16)

    @blk.sync
    def _(sync):
        sync.wait_ge(stw[0], 16 * 8)
        sync.wait_ge(stw[1], 16 * 7)
    return p


def build_l1ffn():
    p = Prog("l1ffn")
    nc = p.nc
    x3_i = nc.dram_tensor("x3_i", [D, TOK], F32, kind="ExternalInput").ap()
    wgu_t = nc.dram_tensor("wgu_t", [88, 128, NKD, 128], BF16, kind="ExternalInput").ap()
    wdn_t = nc.dram_tensor("wdn_t", [16, 128, 44, 128], BF16, kind="ExternalInput").ap()
    out_o = nc.dram_tensor("out_o", [D, TOK], F32, kind="ExternalOutput").ap()

    xch = p.sbuf("xch", [128, NKD, 512], F32)
    sqb = p.sbuf("sqb", [128, 2 * 512], F32)
    xnq = p.sbuf("xnq", [128, NKD, 512], BF16)
    stat_sb = p.sbuf("stat", [1, 6 * 512], F32)
    ones_k = p.sbuf("ones_k", [128, 1], F32)
    ones_1 = p.sbuf("ones_1", [1, 128], F32)
    sg = p.sbuf("sg", [128, 44, 512], BF16)
    hh = p.sbuf("hh", [128, 44, 512], BF16)
    upb = p.sbuf("upb", [128, 512], BF16)
    yfull = p.sbuf("yfull", [128, 16, 512], BF16)
    sqy = p.sbuf("sqy", [128, 512], F32)
    sqy2 = p.sbuf("sqy2", [128, 512], F32)
    rbb = p.sbuf("rbb", [128, 512], BF16)
    wsl = [p.sbuf(f"wsl{i}", [128, NKD * 128], BF16) for i in range(2)]
    wsl2 = [p.sbuf("wsl2_0", [128, 44 * 128], BF16)]
    xres = [p.sbuf(f"xres{i}", [128, 512], F32) for i in range(2)]
    outs = [p.sbuf(f"outs{i}", [128, 512], F32) for i in range(2)]
    banks = [p.psum(f"pb{i}", [128, 512], F32) for i in range(6)]
    bc_ps = p.psum("bc_ps", [128, 1024], F32)

    ld = p.sem("ld"); cl = p.ctr("ld")
    sv, ss, sm = p.sem("sv"), p.sem("ss"), p.sem("sm")
    cv, cs, cm = p.ctr("sv"), p.ctr("ss"), p.ctr("sm")
    lx = [p.sem(f"lx{i}") for i in range(2)]
    sto = [p.sem(f"sto{i}") for i in range(2)]
    sev1 = p.sem("h1_ev"); cev1 = p.ctr("h1_ev")
    sev2 = p.sem("h2_ev"); cev2 = p.ctr("h2_ev")
    sev3 = p.sem("h3_ev"); cev3 = p.ctr("h3_ev")

    blk = p.begin()

    @blk.vector
    def _(vector):
        vector.memset(ones_k[:], 1.0)
        vector.memset(ones_1[:], 1.0).then_inc(sv, 1)
    prev_ln = cv.inc()

    evds2 = {}
    up_done = {}
    for tq in range(4):
        c0 = tq * 512

        @blk.sync
        def _(sync, c0=c0, prev_ln=prev_ln):
            sync.wait_ge(sv, prev_ln)
            sync.dma_start(out=xch[:], in_=x3_i[:, c0:c0 + 512].rearrange(
                "(k p) w -> p k w", p=128)).then_inc(ld, 16)
        xld = cl.inc(16)
        w_pre2 = [(ld, xld), (sv, prev_ln)]
        if tq > 0:
            w_pre2.append((sev3, evds2[tq * 16 - 1]))
        prev_ln = emit_ln(
            p, lambda k: xch[:, k, :], NKD, 512, lambda k: xnq[:, k, :],
            sqb, (banks[0], banks[1]), bc_ps, stat_sb, ones_k, ones_1, D,
            waits_pre=w_pre2)

        def evac1(o, bs, mmd, tq=tq):
            @blk.scalar
            def _(scalar, o=o, bs=bs, mmd=mmd):
                scalar.wait_ge(p.sems["sm"], mmd)
                if o < 44:
                    scalar.activation(sg[:, o, :], bs[0][:, :], AF.Silu
                                      ).then_inc(sev1, 1)
                else:
                    if o > 44:
                        scalar.wait_ge(sev1, up_done[o - 1])
                    scalar.copy(upb[:], bs[0][:, :]).then_inc(sev1, 1)
            r = cev1.inc()
            if o >= 44:
                @blk.vector
                def _(vector, o=o, r=r):
                    vector.wait_ge(sev1, r)
                    vector.tensor_tensor(hh[:, o - 44, :], sg[:, o - 44, :],
                                         upb[:], ALU.mult).then_inc(sev1, 1)
                r = cev1.inc()
                up_done[o] = r
            return r

        emit_mm_stream(p, f"h1_{tq}", wgu_t, 88, NKD, [(0, 512)],
                       lambda k, a, b: xnq[:, k, a:b], evac1,
                       banks, wsl, pre_wait=[(sv, prev_ln)], ev_name="h1_ev")
        mm1_last = cev1.v

        def evac2(o, bs, mmd, tq=tq):
            @blk.scalar
            def _(scalar, o=o, bs=bs, mmd=mmd, tq=tq):
                scalar.wait_ge(p.sems["sm"], mmd)
                if o == 0 and tq > 0:
                    scalar.wait_ge(sev3, evds2[tq * 16 - 1])
                scalar.copy(yfull[:, o, :], bs[0][:, :]).then_inc(sev2, 1)
            return cev2.inc()

        emit_mm_stream(p, f"h2_{tq}", wdn_t, 16, 44, [(0, 512)],
                       lambda k, a, b: hh[:, k, a:b], evac2,
                       banks, wsl2, pre_wait=[(sev1, mm1_last)], nslot=1,
                       ev_name="h2_ev")
        mm2_last = cev2.v

        sq_d = {}
        smm_base2 = None
        for k in range(16):
            @blk.scalar
            def _(scalar, k=k, mm2_last=mm2_last, sb2=smm_base2):
                if k == 0:
                    scalar.wait_ge(sev2, mm2_last)
                if k >= 1:
                    scalar.wait_ge(sm, sb2 + k)
                scalar.activation(sqy[:], yfull[:, k, :], AF.Square).then_inc(ss, 1)
            sq_d[k] = cs.inc()
            if k == 0:
                smm_base2 = p.ctrs["sm"].v

            @blk.tensor
            def _(tensor, k=k, tq=tq):
                tensor.wait_ge(ss, sq_d[k])
                tensor.matmul(banks[0][0:1, :], ones_k[:, 0:1], sqy[:],
                              start=(k == 0), stop=(k == 15)).then_inc(sm, 1)
            cm.inc()
        stats_d = p.ctrs["sm"].v

        @blk.vector
        def _(vector, stats_d=stats_d):
            vector.wait_ge(sm, stats_d)
            vector.tensor_scalar(stat_sb[0:1, 0:512], banks[0][0:1, :],
                                 1.0 / 2048.0, 1e-6, ALU.mult, ALU.add
                                 ).then_inc(sv, 1)
        mu_d = cv.inc()

        @blk.scalar
        def _(scalar, mu_d=mu_d):
            scalar.wait_ge(sv, mu_d)
            scalar.activation(stat_sb[0:1, 512:1024], stat_sb[0:1, 0:512],
                              AF.Sqrt).then_inc(ss, 1)
        sqr_d = cs.inc()

        @blk.vector
        def _(vector, sqr_d=sqr_d):
            vector.wait_ge(ss, sqr_d)
            vector.reciprocal(stat_sb[0:1, 1024:1536], stat_sb[0:1, 512:1024]
                              ).then_inc(sv, 1)
        rec_d = cv.inc()

        @blk.tensor
        def _(tensor, rec_d=rec_d):
            tensor.wait_ge(sv, rec_d)
            tensor.matmul(bc_ps[:, 0:512], ones_1[0:1, :], stat_sb[0:1, 1024:1536],
                          start=True, stop=True).then_inc(sm, 1)
        bcr_d = cm.inc()

        @blk.scalar
        def _(scalar, bcr_d=bcr_d):
            scalar.wait_ge(sm, bcr_d)
            scalar.copy(rbb[:], bc_ps[:, 0:512]).then_inc(ss, 1)
        rb_d = cs.inc()

        for o in range(16):
            g2 = tq * 16 + o

            @blk.sync
            def _(sync, o=o, g2=g2, c0=c0):
                if g2 >= 2:
                    sync.wait_ge(sev3, evds2[g2 - 2])
                sync.dma_start(out=xres[g2 % 2][:],
                               in_=x3_i[o * 128:(o + 1) * 128, c0:c0 + 512]
                               ).then_inc(lx[g2 % 2], 16)

            @blk.vector
            def _(vector, o=o, g2=g2, rb_d=rb_d):
                vector.wait_ge(ss, rb_d)
                vector.wait_ge(lx[g2 % 2], 16 * (g2 // 2 + 1))
                if g2 >= 2:
                    vector.wait_ge(sto[g2 % 2], 16 * (g2 // 2))
                vector.tensor_tensor(sqy2[:], yfull[:, o, :], rbb[:], ALU.mult)
                vector.tensor_add(outs[g2 % 2][:], sqy2[:], xres[g2 % 2][:]
                                  ).then_inc(sev3, 1)
            evds2[g2] = cev3.inc()

            @blk.sync
            def _(sync, o=o, g2=g2, c0=c0):
                sync.wait_ge(sev3, evds2[g2])
                sync.dma_start(out=out_o[o * 128:(o + 1) * 128, c0:c0 + 512],
                               in_=outs[g2 % 2][:]).then_inc(sto[g2 % 2], 16)

    @blk.sync
    def _(sync):
        sync.wait_ge(sto[0], 16 * 32)
        sync.wait_ge(sto[1], 16 * 32)
    return p


def _run(p, in_maps):
    res = run_bass_kernel_spmd(p.nc, in_maps, core_ids=list(range(8)))
    p.close()
    return [dict(r) for r in res.results]


def kernel(**inputs):
    ins = {k: np.asarray(v) for k, v in inputs.items()}

    rA = _run(build_l0a(), [stage_l0a(ins, c) for c in range(8)])

    rB = _run(build_l0scan(), [stage_l0scan(rA, ins, c) for c in range(8)])

    rC = _run(build_l0out(), [stage_l0out(rB, ins, c) for c in range(8)])

    rD = _run(build_l0ffn(), [stage_l0ffn(rC, ins, c) for c in range(8)])

    def stage_e(c):
        b, th = c // 2, c % 2
        x2 = np.concatenate([rD[2 * b]["x2_o"], rD[2 * b + 1]["x2_o"]], axis=1)
        s = th * TOK
        if th == 0:
            xhv = np.concatenate([np.zeros((D, 3), np.float32), x2[:, 0:TOK]], axis=1)
        else:
            xhv = x2[:, s - 3:s + TOK]
        return {
            "xh": np.ascontiguousarray(xhv),
            "w_ip": tile_w(ins["L1_attn_in_proj_w"], pad_to=8448),
            "cw": np.ascontiguousarray(
                ins["L1_attn_conv_w"].reshape(33, 128, 4).transpose(1, 0, 2)),
            "cb": tile_vec(ins["L1_attn_conv_b"]),
            "dtb": np.ascontiguousarray(
                np.pad(ins["L1_attn_dt_bias"], (0, 96)).reshape(128, 1)),
        }
    rE = _run(build_l1pre(), [stage_e(c) for c in range(8)])

    mask = np.zeros((128, 2, 256), np.float32)
    for sc in range(2):
        for pp in range(128):
            mask[pp, sc, sc * 128 + pp:] = 1.0
    eh128 = np.zeros((16, 16 * 128), np.float32)
    eh64 = np.zeros((16, 16 * 64), np.float32)
    for h in range(16):
        eh128[h, h * 128:(h + 1) * 128] = 1.0
        eh64[h, h * 64:(h + 1) * 64] = 1.0
    eye128 = np.eye(128, dtype=np.float32)

    def stage_f(c):
        b, hh = c // 2, c % 2
        xsc = np.concatenate([rE[2 * b]["xsc_o"], rE[2 * b + 1]["xsc_o"]], axis=1)
        dt1 = np.concatenate([rE[2 * b]["dt1_o"], rE[2 * b + 1]["dt1_o"]], axis=1)
        rs = slice(hh * 2048, (hh + 1) * 2048)
        bcf = xsc[4096:4224]
        return {
            "xs_i": np.ascontiguousarray(xsc[0:4096][rs].T),
            "bct_i": np.ascontiguousarray(bcf.T),
            "bcf_i": np.ascontiguousarray(bcf),
            "dt_i": np.ascontiguousarray(dt1[hh * 16:(hh + 1) * 16]),
            "alog_i": np.ascontiguousarray(
                ins["L1_attn_A_log"][hh * 16:(hh + 1) * 16, None]),
            "dbc_i": np.ascontiguousarray(
                np.repeat(ins["L1_attn_D"][None, hh * 16:(hh + 1) * 16], 128, 0)),
            "mask_i": mask, "eh128_i": eh128, "eh64_i": eh64, "ident_i": eye128,
        }
    rF = _run(build_l1ssd(), [stage_f(c) for c in range(8)])

    def stage_g(c):
        b, th = c // 2, c % 2
        s = th * TOK
        yfull = np.concatenate([rF[2 * b]["y_o"], rF[2 * b + 1]["y_o"]], axis=1)
        x2 = rD[c]["x2_o"]
        return {
            "y_i": np.ascontiguousarray(yfull[s:s + TOK].T),
            "z_i": np.ascontiguousarray(rE[c]["z1_o"]),
            "nw_i": tile_vec(ins["L1_attn_norm_w"]),
            "w_op": tile_w(ins["L1_attn_out_proj_w"]),
            "x2_i": np.ascontiguousarray(x2),
        }
    rG = _run(build_l1out(), [stage_g(c) for c in range(8)])

    def stage_h1(c):
        rs = slice(c * 1408, (c + 1) * 1408)
        ds_ = slice(c * 256, (c + 1) * 256)
        return {
            "gpk": np.ascontiguousarray(ins["L1_ffn_gate_up_packed"][rs]),
            "gsc": np.ascontiguousarray(ins["L1_ffn_gate_up_scales"].T[rs]),
            "dpk": np.ascontiguousarray(ins["L1_ffn_down_packed"][ds_]),
            "dsc": np.ascontiguousarray(ins["L1_ffn_down_scales"].T[ds_]),
        }
    rH1 = _run(build_fp4deq(), [stage_h1(c) for c in range(8)])

    wgu = np.concatenate([rH1[c]["wgu_o"] for c in range(8)], axis=0)
    wdn = np.concatenate([rH1[c]["wdn_o"] for c in range(8)], axis=0)
    wgu_t = tile_w(wgu)
    wdn_t = tile_w(wdn)

    def stage_h2(c):
        return {
            "x3_i": np.ascontiguousarray(rG[c]["x3_o"]),
            "wgu_t": wgu_t,
            "wdn_t": wdn_t,
        }
    rH2 = _run(build_l1ffn(), [stage_h2(c) for c in range(8)])

    out = np.empty((B, L, D), np.float32)
    for c in range(8):
        b, th = c // 2, c % 2
        s = th * TOK
        out[b, s:s + TOK, :] = rH2[c]["out_o"].T
    return out
